# version 24
# baseline (speedup 1.0000x reference)
import contextlib
import numpy as np
import concourse.bass as bass
import concourse.mybir as mybir
from concourse.bass_utils import run_bass_kernel_spmd

F32 = mybir.dt.float32
F16 = mybir.dt.float16
AF = mybir.ActivationFunctionType
OP = mybir.AluOpType
AX = mybir.AxisListType

B, T, D = 16, 2048, 1024
H, DH, LR = 8, 256, 64
HS, E, HR = 2048, 8, 1024
NSEQ = 2
NTOK = NSEQ * T
SCALE = 1.0 / 16.0

ENGS = ("sync", "vector", "scalar", "tensor", "gpsimd")
NDMA_SEM = 8


class Slab:

    def __init__(self, h, base, size=None):
        self.h, self.base, self.size = h, base, size

    def __getitem__(self, idx):
        r, c = idx
        stop = c.stop if c.stop is not None else self.size
        return self.h[r, (self.base + (c.start or 0)):(self.base + stop)]


class Bld:
    def __init__(self):
        self.nc = bass.Bass("TRN2", target_bir_lowering=False)
        self.es = contextlib.ExitStack()
        self.ct = {}
        self.sems = {}
        self.ph = None
        self.off = False
        self.didx = 0
        self.gidx = 0
        self.last = {"vector": 0, "scalar": 0}
        self.es_hist = []

    def sem(self, name):
        h = self.es.enter_context(self.nc.semaphore(name))
        self.sems[name] = h
        self.ct[name] = 0

    def sbuf(self, name, shape, scope=None, dtype=F32):
        return (scope or self.es).enter_context(self.nc.sbuf_tensor(name, shape, dtype))

    def psum(self, name, shape, scope=None):
        return (scope or self.es).enter_context(self.nc.psum_tensor(name, shape, F32))

    def new_phase(self):
        self.ph = {k: [] for k in ENGS}

    def q(self, eng, method, *args, wait=(), inc=None, **kw):
        if self.off:
            return self.ct[inc[0]] if inc else None
        wait = list(wait)
        if eng == "vector":
            wait.append(("sv", self.last["vector"]))
            if inc is None:
                inc = ("sv", 1)
        elif eng == "scalar":
            wait.append(("ss", self.last["scalar"]))
            if inc is None:
                inc = ("ss", 1)
        wl = [(self.sems[s], v) for (s, v) in wait if v > 0]
        tgt = None
        incspec = None
        if inc is not None:
            s, amt = inc
            self.ct[s] += amt
            tgt = self.ct[s]
            incspec = (self.sems[s], amt)
        if eng == "vector":
            self.last["vector"] = tgt
        elif eng == "scalar":
            self.last["scalar"] = tgt

        def fn(Eng, method=method, args=args, kw=kw, wl=wl, incspec=incspec):
            for (h, v) in wl:
                Eng.wait_ge(h, v)
            inst = getattr(Eng, method)(*args, **kw)
            if incspec is not None:
                inst.then_inc(*incspec)

        self.ph[eng].append(fn)
        return tgt

    def dma(self, eng, out, in_, wait=(), **kw):
        if self.off:
            return ("d0", 0)
        if eng == "gpsimd":
            name = f"g{self.gidx % 8}"
            self.gidx += 1
        else:
            name = f"d{self.didx % NDMA_SEM}"
            self.didx += 1
        wl = list(wait)
        if self.ct[name] > 0:
            wl.append((name, self.ct[name]))
        tgt = self.q(eng, "dma_start", out=out, in_=in_, wait=wl, inc=(name, 16), **kw)
        return (name, tgt)

    def esplit(self, ps_slice, hi_ap, lo_ap, wait=()):
        wait = list(wait)
        if len(self.es_hist) >= 2:
            wait.append(("sv", self.es_hist[-2]))
        ss_t = self.q("scalar", "copy", hi_ap, ps_slice, wait=wait)
        sv_t = self.q("vector", "tensor_tensor", lo_ap, ps_slice, hi_ap,
                      OP.subtract, wait=[("ss", ss_t)])
        self.es_hist.append(sv_t)
        return ss_t, sv_t

    def qwait(self, eng, wait):
        if self.off:
            return
        wl = [(self.sems[s], v) for (s, v) in wait if v > 0]

        def fn(Eng, wl=wl):
            for (h, v) in wl:
                Eng.wait_ge(h, v)

        self.ph[eng].append(fn)

    def flush(self, name):
        ph = self.ph
        self.qwait("vector", [("sv", self.ct["sv"])])
        self.qwait("scalar", [("ss", self.ct["ss"])])
        with self.nc.Block(name) as blk:
            @blk.sync
            def _(Eng):
                for f in ph["sync"]:
                    f(Eng)

            @blk.vector
            def _(Eng):
                for f in ph["vector"]:
                    f(Eng)

            @blk.scalar
            def _(Eng):
                for f in ph["scalar"]:
                    f(Eng)

            @blk.tensor
            def _(Eng):
                for f in ph["tensor"]:
                    f(Eng)

            @blk.gpsimd
            def _(Eng):
                for f in ph["gpsimd"]:
                    f(Eng)
        self.ph = None


def moe_jobs(half):
    jobs = {}
    for r in range(half * 4, half * 4 + 4):
        for e in ((3 * r) % 8, (3 * r + 7) % 8):
            jobs.setdefault(e, []).append(r)
    return sorted(jobs.items())


def build_nc(stages=("att", "ffnsh", "moe")):
    b = Bld()
    nc = b.nc

    xd = nc.dram_tensor("x", [NTOK, D], F32, kind="ExternalInput")
    yd = nc.dram_tensor("y", [NTOK, D], F32, kind="ExternalOutput")
    identd = nc.dram_tensor("ident", [128, 128], F32, kind="ExternalInput")
    wq_d = nc.dram_tensor("Wq", [H * DH, D], F32, kind="ExternalInput")
    wkv_d = nc.dram_tensor("Wkv", [LR, D], F32, kind="ExternalInput")
    wku_d = nc.dram_tensor("Wku", [H * DH, LR], F32, kind="ExternalInput")
    wvu_d = nc.dram_tensor("Wvu", [H * DH, LR], F32, kind="ExternalInput")
    wo_d = nc.dram_tensor("Wo", [D, H * DH], F32, kind="ExternalInput")
    sw1_d = nc.dram_tensor("sw1", [D, HS], F32, kind="ExternalInput")
    sb1_d = nc.dram_tensor("sb1", [HS], F32, kind="ExternalInput")
    sw3_d = nc.dram_tensor("sw3", [D, HS], F32, kind="ExternalInput")
    sb3_d = nc.dram_tensor("sb3", [HS], F32, kind="ExternalInput")
    sw2_d = nc.dram_tensor("sw2", [HS, D], F32, kind="ExternalInput")
    sb2_d = nc.dram_tensor("sb2", [D], F32, kind="ExternalInput")
    w1_d = nc.dram_tensor("w1", [E, D, HR], F32, kind="ExternalInput")
    b1_d = nc.dram_tensor("b1", [E, HR], F32, kind="ExternalInput")
    w2_d = nc.dram_tensor("w2", [E, HR, D], F32, kind="ExternalInput")
    b2_d = nc.dram_tensor("b2", [E, D], F32, kind="ExternalInput")
    wqTh_d = nc.dram_tensor("wqTh_s", [H * 128, 2 * D], F16, kind="Internal")
    wqTl_d = nc.dram_tensor("wqTl_s", [H * 128, 2 * D], F16, kind="Internal")
    woTh_d = nc.dram_tensor("woTh_s", [H * 128, 2 * D], F16, kind="Internal")
    woTl_d = nc.dram_tensor("woTl_s", [H * 128, 2 * D], F16, kind="Internal")
    sw1h_d = nc.dram_tensor("sw1h_s", [D, HS], F16, kind="Internal")
    sw1l_d = nc.dram_tensor("sw1l_s", [D, HS], F16, kind="Internal")
    sw3h_d = nc.dram_tensor("sw3h_s", [D, HS], F16, kind="Internal")
    sw3l_d = nc.dram_tensor("sw3l_s", [D, HS], F16, kind="Internal")
    sw2h_d = nc.dram_tensor("sw2h_s", [HS, D], F16, kind="Internal")
    sw2l_d = nc.dram_tensor("sw2l_s", [HS, D], F16, kind="Internal")
    w1h_d = nc.dram_tensor("w1h_s", [E, D, HR], F16, kind="Internal")
    w1l_d = nc.dram_tensor("w1l_s", [E, D, HR], F16, kind="Internal")
    w2h_d = nc.dram_tensor("w2h_s", [E, HR, D], F16, kind="Internal")
    w2l_d = nc.dram_tensor("w2l_s", [E, HR, D], F16, kind="Internal")
    b2h_d = nc.dram_tensor("b2h_s", [E, D], F16, kind="Internal")
    b2l_d = nc.dram_tensor("b2l_s", [E, D], F16, kind="Internal")
    sb2h_d = nc.dram_tensor("sb2h_s", [D], F16, kind="Internal")
    sb2l_d = nc.dram_tensor("sb2l_s", [D], F16, kind="Internal")

    x = xd.ap()
    ident = identd.ap()
    wkv = wkv_d.ap()

    for s in ("so", "sv", "ss", "st"):
        b.sem(s)
    for i in range(NDMA_SEM):
        b.sem(f"d{i}")
    for i in range(8):
        b.sem(f"g{i}")

    xnh = b.sbuf("xnh", [128, 8 * T], dtype=F16)
    xnl = b.sbuf("xnl", [128, 8 * T], dtype=F16)
    acc = b.sbuf("acc", [128, 8 * D])
    kvh = b.sbuf("kvh", [LR, T], dtype=F16)
    kvl = b.sbuf("kvl", [LR, T], dtype=F16)
    idn = b.sbuf("idn", [128, 128])
    lnx = b.sbuf("lnx", [128, D])
    xntmp = b.sbuf("xntmp", [128, D])
    misc = b.sbuf("misc", [128, 128])
    cols = Slab(misc, 0, 8)
    sums = Slab(misc, 8, 4)
    sb1c = Slab(misc, 16, 16)
    sb3c = Slab(misc, 32, 16)
    b1c = Slab(misc, 48, 64)
    lncols = Slab(misc, 112, 16)
    wkvT2 = b.sbuf("wkvT2", [128, 16 * LR], dtype=F16)
    wkvTh = Slab(wkvT2, 0, 8 * LR)
    wkvTl = Slab(wkvT2, 8 * LR, 8 * LR)
    wkuTh = b.sbuf("wkuTh", [LR, H * DH], dtype=F16)
    wkuTl = b.sbuf("wkuTl", [LR, H * DH], dtype=F16)
    wvuTh = b.sbuf("wvuTh", [LR, H * DH], dtype=F16)
    wvuTl = b.sbuf("wvuTl", [LR, H * DH], dtype=F16)

    wfsc = contextlib.ExitStack()
    NWF = 4
    wfAs = [b.sbuf(f"wfA{i}", [128, 1024], scope=wfsc) for i in range(NWF)]
    wfHs = [b.sbuf(f"wfH{i}", [128, 1024], scope=wfsc, dtype=F16) for i in range(NWF)]
    wfLs = [b.sbuf(f"wfL{i}", [128, 1024], scope=wfsc, dtype=F16) for i in range(NWF)]
    wf_chunks = []
    for src_d, dsth_d, dstl_d, nrow in ((sw1_d, sw1h_d, sw1l_d, 8),
                                        (sw3_d, sw3h_d, sw3l_d, 8)):
        for c in range(nrow):
            for hh in range(2):
                wf_chunks.append((src_d.ap()[c * 128:(c + 1) * 128, hh * 1024:(hh + 1) * 1024],
                                  dsth_d.ap()[c * 128:(c + 1) * 128, hh * 1024:(hh + 1) * 1024],
                                  dstl_d.ap()[c * 128:(c + 1) * 128, hh * 1024:(hh + 1) * 1024]))
    for c in range(16):
        wf_chunks.append((sw2_d.ap()[c * 128:(c + 1) * 128, :],
                          sw2h_d.ap()[c * 128:(c + 1) * 128, :],
                          sw2l_d.ap()[c * 128:(c + 1) * 128, :]))
    wf_chunks.append((b2_d.ap()[:, :], b2h_d.ap()[:, :], b2l_d.ap()[:, :], 8))
    wf_chunks.append((bass.AP(sb2_d, 0, [[1, 1], [1, D]]),
                      bass.AP(sb2h_d, 0, [[1, 1], [1, D]]),
                      bass.AP(sb2l_d, 0, [[1, 1], [1, D]]), 1))
    for e in range(E):
        for c in range(8):
            wf_chunks.append((w1_d.ap()[e, c * 128:(c + 1) * 128, :],
                              w1h_d.ap()[e, c * 128:(c + 1) * 128, :],
                              w1l_d.ap()[e, c * 128:(c + 1) * 128, :]))
        for c in range(8):
            wf_chunks.append((w2_d.ap()[e, c * 128:(c + 1) * 128, :],
                              w2h_d.ap()[e, c * 128:(c + 1) * 128, :],
                              w2l_d.ap()[e, c * 128:(c + 1) * 128, :]))
    wf_state = {"i": 0, "tok": [{"h": ("d0", 0), "l": ("d0", 0),
                                 "ss": 0, "sv": 0} for _ in range(NWF)]}

    def wf_emit(n):
        for _ in range(n):
            i = wf_state["i"]
            if i >= len(wf_chunks):
                return
            wf_state["i"] += 1
            ck = wf_chunks[i]
            (src, dsth, dstl), nr = ck[:3], (ck[3] if len(ck) > 3 else 128)
            par = i % NWF
            tk = wf_state["tok"][par]
            wfA, wfH, wfL = wfAs[par], wfHs[par], wfLs[par]
            t_in = b.dma("sync", out=wfA[0:nr, :], in_=src,
                         wait=[w for w in (("ss", tk["ss"]), ("sv", tk["sv"])) if w[1] > 0])
            ss_t = b.q("scalar", "copy", wfH[0:nr, :], wfA[0:nr, :],
                       wait=[t_in, tk["h"]])
            sv_t = b.q("vector", "tensor_tensor", wfL[0:nr, :], wfA[0:nr, :],
                       wfH[0:nr, :], OP.subtract, wait=[("ss", ss_t), tk["l"]])
            th = b.dma("gpsimd", out=dsth, in_=wfH[0:nr, :], wait=[("ss", ss_t)])
            tl = b.dma("gpsimd", out=dstl, in_=wfL[0:nr, :], wait=[("sv", sv_t)])
            wf_state["tok"][par] = {"h": th, "l": tl, "ss": ss_t, "sv": sv_t}

    wsc = contextlib.ExitStack()
    ps_w2 = b.psum("ps_w2", [128, 2048], scope=wsc)
    ps_w3 = b.psum("ps_w3", [128, 2048], scope=wsc)
    stgAs = (b.sbuf("stgA0", [128, 2048], scope=wsc), b.sbuf("stgA1", [128, 2048], scope=wsc))
    stgHs = (b.sbuf("stgH0", [128, 2048], scope=wsc, dtype=F16),
             b.sbuf("stgH1", [128, 2048], scope=wsc, dtype=F16))
    stgLs = (b.sbuf("stgL0", [128, 2048], scope=wsc, dtype=F16),
             b.sbuf("stgL1", [128, 2048], scope=wsc, dtype=F16))
    b.new_phase()
    t_id = b.dma("sync", out=idn[:, :], in_=ident[:, :])
    t_kv = b.dma("sync", out=lnx[0:LR, :], in_=wkv[:, :])
    b.q("vector", "memset", cols[:, 7:8], 0.0)
    b.dma("sync", out=sb1c[:, :], in_=bass.AP(sb1_d, 0, [[1, 128], [128, 16]]),
          allow_slow_non_contiguous=True)
    b.dma("sync", out=sb3c[:, :], in_=bass.AP(sb3_d, 0, [[1, 128], [128, 16]]),
          allow_slow_non_contiguous=True)
    for e in range(E):
        b.dma("sync", out=b1c[:, e * 8:(e + 1) * 8],
              in_=bass.AP(b1_d, e * HR, [[1, 128], [128, 8]]),
              allow_slow_non_contiguous=True)
    b.q("vector", "memset", cols[:, 6:7], 1e-5)
    st_w = None
    for c in range(8):
        st_w = b.q("tensor", "transpose", ps_w2[:, c * 64:(c + 1) * 64],
                   lnx[0:LR, c * 128:(c + 1) * 128], idn[0:LR, 0:LR],
                   wait=[t_id, t_kv] if c == 0 else (), inc=("st", 1))
    ss_wkv, v_wkv = b.esplit(ps_w2[:, 0:512], wkvTh[:, :], wkvTl[:, :],
                             wait=[("st", st_w)])

    wstate = {"st_stgA": [0, 0], "t_out": [[("d0", 0)], [("d0", 0)]],
              "ev": [[("sv", 0)], []], "item": 0}

    def w_prep(load_in, npart, tiles, out_h, out_l, load_cols=2048, split=True):
        par = wstate["item"] % 2
        wstate["item"] += 1
        stgA = stgAs[par]
        psw = (ps_w2, ps_w3)[par]
        if wstate["item"] == 1:
            wstate["ev"][0] = [("sv", v_wkv)]
        t_ld = b.dma("sync", out=stgA[:, 0:load_cols], in_=load_in,
                     wait=[("st", wstate["st_stgA"][par])] if wstate["st_stgA"][par] else ())
        st_t = {}
        iw = 64 if npart == LR else 128
        for i, (icol, ocol) in enumerate(tiles):
            w8 = [t_ld] + wstate["ev"][par] if i == 0 else ()
            tgt = b.q("tensor", "transpose", psw[0:npart, i * 128:(i + 1) * 128],
                      stgA[:, icol:icol + iw], idn[:, :],
                      wait=w8, inc=("st", 1))
            st_t[i] = tgt
        wstate["st_stgA"][par] = st_t[len(tiles) - 1]
        nt = len(tiles)
        nbank = (nt + 3) // 4
        order = [4 * bk + k for k in range(4) for bk in range(nbank) if 4 * bk + k < nt]
        evs = []
        if split:
            stgH, stgL = stgHs[par], stgLs[par]
            for ii, i in enumerate(order):
                ocol = tiles[i][1]
                w8 = [("st", st_t[min((i // 4) * 4 + 3, nt - 1)])]
                if ii == 0:
                    w8 += wstate["t_out"][par]
                ss_t, sv_t = b.esplit(psw[0:npart, i * 128:(i + 1) * 128],
                                      stgH[0:npart, ocol:ocol + 128],
                                      stgL[0:npart, ocol:ocol + 128], wait=w8)
                evs.append(("sv", sv_t))
            ev2 = [("sv", max(v for _, v in evs))]
            wstate["ev"][par] = ev2
            th = b.dma("sync", out=out_h, in_=stgH[0:npart, :], wait=ev2)
            tl = b.dma("sync", out=out_l, in_=stgL[0:npart, :], wait=ev2)
            wstate["t_out"][par] = [th, tl]
        else:
            for ii, i in enumerate(order):
                ocol = tiles[i][1]
                w8 = [("st", st_t[min((i // 4) * 4 + 3, nt - 1)])]
                _, sv_t = b.esplit(psw[0:npart, i * 128:(i + 1) * 128],
                                   out_h[0:npart, ocol:ocol + 128],
                                   out_l[0:npart, ocol:ocol + 128], wait=w8)
                evs.append(("sv", sv_t))
            ev2 = [("sv", max(v for _, v in evs))]
            wstate["ev"][par] = ev2
            wstate["t_out"][par] = []

    for h in range(H):
        w_prep(bass.AP(wq_d, h * DH * D, [[1024, 128], [128 * 1024, 2], [1, 1024]]), 128,
               [(dqc * 1024 + c * 128, c * DH + dqc * 128) for dqc in range(2) for c in range(8)],
               wqTh_d.ap()[h * 128:(h + 1) * 128, :], wqTl_d.ap()[h * 128:(h + 1) * 128, :])
        w_prep(bass.AP(wo_d, h * DH, [[2048, 128], [128 * 2048, 8], [1, DH]]), 128,
               [(j * DH + dc * 128, dc * D + j * 128) for j in range(8) for dc in range(2)],
               woTh_d.ap()[h * 128:(h + 1) * 128, :], woTl_d.ap()[h * 128:(h + 1) * 128, :])
    w_prep(bass.AP(wku_d, 0, [[64, 128], [128 * 64, 16], [1, 64]]), LR,
           [(i * 64, i * 128) for i in range(16)], wkuTh, wkuTl,
           load_cols=1024, split=False)
    w_prep(bass.AP(wvu_d, 0, [[64, 128], [128 * 64, 16], [1, 64]]), LR,
           [(i * 64, i * 128) for i in range(16)], wvuTh, wvuTl,
           load_cols=1024, split=False)

    b.qwait("sync", [(f"d{i}", b.ct[f"d{i}"]) for i in range(NDMA_SEM)])
    b.qwait("sync", [(f"g{i}", b.ct[f"g{i}"]) for i in range(8)])
    b.qwait("sync", [("sv", b.ct["sv"]), ("ss", b.ct["ss"])])
    b.flush("W")
    wsc.close()

    out_total = [0]
    acc1_tok = ("d0", 0)
    pending_out = [None]
    acc1_box = [("d0", 0)]

    def emit_out(s_, half_):
        row0 = s_ * T + half_ * 1024
        out_total[0] = b.q("sync", "dma_start", inc=("so", 16),
                           wait=[("so", out_total[0])],
                           out=bass.AP(yd, row0 * D, [[1024, 128], [128 * 1024, 8], [1, 1024]]),
                           in_=acc[:, :])
        if half_ == 0:
            row1 = s_ * T + 1024
            acc1_box[0] = b.dma("sync", out=acc[:, :],
                                in_=bass.AP(xd, row1 * D, [[1024, 128], [128 * 1024, 8], [1, 1024]]),
                                wait=[("so", out_total[0])])
        else:
            acc1_box[0] = ("d0", 0)
        pending_out[0] = None

    for s in range(NSEQ):
        lnsc = contextlib.ExitStack()
        ps_ln = b.psum(f"ps_ln{s}", [128, 1024], scope=lnsc)
        ps_kv = b.psum(f"ps_kv{s}", [LR, 128], scope=lnsc)
        b.new_phase()
        if pending_out[0] is not None:
            emit_out(*pending_out[0])
        v_evac_prev = 0
        lnx_free = 0
        st_b_prev = 0
        v_kv_prev = 0
        for t in range(16):
            C = lncols
            tok = t * 128
            t_x = b.dma("sync", out=lnx[:, :], in_=x[s * T + tok: s * T + tok + 128, :],
                        wait=[("sv", lnx_free)])
            w1 = [t_x]
            if st_b_prev:
                w1.append(("st", st_b_prev))
            v1 = b.q("vector", "tensor_scalar", xntmp[:, :], lnx[:, :], 1.0, 0.0, OP.mult, OP.add,
                     accum_out=C[:, 0:1], wait=w1, inc=("sv", 1))
            v2 = b.q("vector", "tensor_tensor", xntmp[:, :], lnx[:, :], lnx[:, :], OP.mult,
                     wait=[("sv", v1)], inc=("sv", 1))
            v3 = b.q("vector", "tensor_scalar", xntmp[:, :], xntmp[:, :], 1.0, 0.0, OP.mult, OP.add,
                     accum_out=C[:, 1:2], wait=[("sv", v2)], inc=("sv", 1))
            v4 = b.q("vector", "tensor_scalar", C[:, 2:3], C[:, 0:1], -1.0 / D, None, OP.mult,
                     wait=[("sv", v3)], inc=("sv", 1))
            v5 = b.q("vector", "tensor_scalar", C[:, 3:4], C[:, 1:2], 1.0 / D, None, OP.mult,
                     wait=[("sv", v3)], inc=("sv", 1))
            b.q("vector", "tensor_tensor", C[:, 4:5], C[:, 2:3], C[:, 2:3], OP.mult,
                wait=[("sv", v4)])
            v_var = b.q("vector", "tensor_tensor", C[:, 3:4], C[:, 3:4], C[:, 4:5],
                        OP.subtract, wait=[("sv", v5)], inc=("sv", 1))
            s_sqrt = b.q("scalar", "activation", C[:, 5:6], C[:, 3:4], AF.Sqrt,
                         bias=cols[:, 6:7], wait=[("sv", v_var)], inc=("ss", 1))
            vr = b.q("vector", "reciprocal", C[:, 5:6], C[:, 5:6], wait=[("ss", s_sqrt)],
                     inc=("sv", 1))
            v_xn = b.q("vector", "tensor_scalar", xntmp[:, :], lnx[:, :], C[:, 2:3],
                       C[:, 5:6], OP.add, OP.mult, wait=[("sv", vr)], inc=("sv", 1))
            if t < 8:
                lnx_free = b.q("vector", "tensor_copy", acc[:, t * D:(t + 1) * D], lnx[:, :],
                               wait=[("so", out_total[0]), ("sv", v_xn)], inc=("sv", 1))
            else:
                lnx_free = v_xn
            st_b = []
            for c in range(8):
                w8 = [("sv", v_xn)]
                if c == 0:
                    w8 += [("sv", v_evac_prev)]
                tgt = b.q("tensor", "transpose", ps_ln[:, c * 128:(c + 1) * 128],
                          xntmp[:, c * 128:(c + 1) * 128], idn[:, :], wait=w8,
                          inc=("st", 1) if c in (3, 7) else None)
                if tgt is not None:
                    st_b.append(tgt)
            st_b_prev = st_b[1]
            v_evac = 0
            xrow = 8 * T
            for bk in range(2):
                w8 = [("st", st_b[bk])]
                c0 = bk * 4
                _, v_evac = b.esplit(
                    ps_ln[:, c0 * 128:(c0 + 4) * 128],
                    bass.AP(xnh, c0 * T + tok, [[xrow, 128], [T, 4], [1, 128]]),
                    bass.AP(xnl, c0 * T + tok, [[xrow, 128], [T, 4], [1, 128]]),
                    wait=w8)
            v_evac_prev = v_evac
            st_kv = None
            nmm = 0
            for c in range(8):
                for (wt, xt_) in ((wkvTh, xnh), (wkvTh, xnl), (wkvTl, xnh)):
                    w8 = []
                    if nmm == 0:
                        w8 = [("sv", max(v_evac, v_kv_prev))]
                    st_kv = b.q("tensor", "matmul", ps_kv[:, :], wt[:, c * LR:(c + 1) * LR],
                                xt_[:, c * T + tok: c * T + tok + 128], wait=w8,
                                start=(nmm == 0), stop=(nmm == 23),
                                inc=("st", 1) if nmm == 23 else None)
                    nmm += 1
            _, v_kv_prev = b.esplit(ps_kv[:, :], kvh[0:LR, tok:tok + 128],
                                    kvl[0:LR, tok:tok + 128], wait=[("st", st_kv)])
            if s == 0:
                wf_emit(12)
        if s == 0:
            wf_emit(len(wf_chunks))
            b.qwait("sync", [(f"d{i}", b.ct[f"d{i}"]) for i in range(NDMA_SEM)])
            b.qwait("sync", [(f"g{i}", b.ct[f"g{i}"]) for i in range(8)])
        b.flush(f"LN{s}")
        lnsc.close()
        if s == 0:
            wfsc.close()

        for half in range(2):
            hbase = half * 1024
            acc1_tok = acc1_box[0]
            asc = contextlib.ExitStack()
            Kfh = b.sbuf(f"Kfh{s}{half}", [128, 2 * T], scope=asc, dtype=F16)
            Kfl = b.sbuf(f"Kfl{s}{half}", [128, 2 * T], scope=asc, dtype=F16)
            Vth = b.sbuf(f"Vth{s}{half}", [128, 16 * DH], scope=asc, dtype=F16)
            Vtl = b.sbuf(f"Vtl{s}{half}", [128, 16 * DH], scope=asc, dtype=F16)
            Pb = b.sbuf(f"Pb{s}{half}", [128, T], scope=asc)
            wqTh = b.sbuf(f"wqTh{s}{half}", [128, 8 * DH], scope=asc, dtype=F16)
            wqTl = b.sbuf(f"wqTl{s}{half}", [128, 8 * DH], scope=asc, dtype=F16)
            woTh = b.sbuf(f"woTh{s}{half}", [128, 2 * D], scope=asc, dtype=F16)
            woTl = b.sbuf(f"woTl{s}{half}", [128, 2 * D], scope=asc, dtype=F16)
            Qfh = b.sbuf(f"Qfh{s}{half}", [128, 2 * 1024], scope=asc, dtype=F16)
            Qfl = b.sbuf(f"Qfl{s}{half}", [128, 2 * 1024], scope=asc, dtype=F16)
            PTh = b.sbuf(f"PTh{s}{half}", [128, T], scope=asc, dtype=F16)
            PTl = b.sbuf(f"PTl{s}{half}", [128, T], scope=asc, dtype=F16)
            osh = b.sbuf(f"osh{s}{half}", [128, DH], scope=asc, dtype=F16)
            osl = b.sbuf(f"osl{s}{half}", [128, DH], scope=asc, dtype=F16)
            ps_S = b.psum(f"psS{s}{half}", [128, 2048], scope=asc)
            ps_T = b.psum(f"psT{s}{half}", [128, 2048], scope=asc)

            b.off = "att" not in stages
            b.new_phase()
            if pending_out[0] is not None and not b.off:
                emit_out(*pending_out[0])
                acc1_tok = acc1_box[0]
            b.flush(f"ATP{s}{half}")

            for h in (range(H) if "att" in stages else ()):
                b.new_phase()
                t_wqh = b.dma("sync", out=wqTh[:, :], in_=wqTh_d.ap()[h * 128:(h + 1) * 128, :])
                t_wql = b.dma("sync", out=wqTl[:, :], in_=wqTl_d.ap()[h * 128:(h + 1) * 128, :])
                t_woh = b.dma("sync", out=woTh[:, :], in_=woTh_d.ap()[h * 128:(h + 1) * 128, :])
                t_wol = b.dma("sync", out=woTl[:, :], in_=woTl_d.ap()[h * 128:(h + 1) * 128, :])
                vk_tok = [0, 0]
                vv_tok = [0, 0]
                qf_tok = [0] * 4

                def kfm_wave(dk):
                    w8 = [("sv", vk_tok[0])] if dk == 1 else []
                    wcol = (2 * h + dk) * 128
                    last = 0
                    for kt in range(4):
                        st_k = None
                        nmm = 0
                        for (wt, kt_) in ((wkuTh, kvh), (wkuTh, kvl), (wkuTl, kvh)):
                            st_k = b.q("tensor", "matmul", ps_S[:, kt * 512:(kt + 1) * 512],
                                       wt[:, wcol:wcol + 128],
                                       kt_[:, kt * 512:(kt + 1) * 512],
                                       wait=w8 if (kt, nmm) == (0, 0) else (),
                                       start=(nmm == 0), stop=(nmm == 2),
                                       inc=("st", 1) if nmm == 2 else None)
                            nmm += 1
                        col = dk * T + kt * 512
                        _, last = b.esplit(ps_S[:, kt * 512:(kt + 1) * 512],
                                           Kfh[:, col:col + 512], Kfl[:, col:col + 512],
                                           wait=[("st", st_k)])
                    vk_tok[dk] = last

                def vtm_wave(wave):
                    w8 = [("sv", vv_tok[0])] if wave == 1 else []
                    st_v = {}
                    for i in range(8):
                        kt16 = wave * 8 + i
                        kcol = kt16 * 128
                        nmm = 0
                        for (kt_, wt) in ((kvh, wvuTh), (kvh, wvuTl), (kvl, wvuTh)):
                            st_v[i] = b.q("tensor", "matmul", ps_T[:, i * DH:(i + 1) * DH],
                                          kt_[:, kcol:kcol + 128],
                                          wt[:, h * DH:(h + 1) * DH],
                                          wait=w8 if (i, nmm) == (0, 0) else (),
                                          start=(nmm == 0), stop=(nmm == 2),
                                          inc=("st", 1) if (i % 2 == 1 and nmm == 2) else None)
                            nmm += 1
                    last = 0
                    for i in range(4):
                        col = (wave * 8 + i * 2) * DH
                        _, last = b.esplit(ps_T[:, i * 2 * DH:(i * 2 + 2) * DH],
                                           Vth[:, col:col + 512], Vtl[:, col:col + 512],
                                           wait=[("st", st_v[i * 2 + 1])])
                    vv_tok[wave] = last

                def qfm_group(g):
                    dqc, tt = g // 2, g % 2
                    st_qf = None
                    nmm = 0
                    for c in range(8):
                        for (wt, xt_) in ((wqTh, xnh), (wqTh, xnl), (wqTl, xnh)):
                            w8 = []
                            if g == 0 and nmm == 0:
                                w8 = [("sv", vk_tok[1]), t_wqh, t_wql]
                            st_qf = b.q("tensor", "matmul",
                                        ps_S[:, g * 512:(g + 1) * 512],
                                        wt[:, c * DH + dqc * 128: c * DH + dqc * 128 + 128],
                                        xt_[:, c * T + hbase + tt * 512: c * T + hbase + tt * 512 + 512],
                                        wait=w8, start=(nmm == 0), stop=(nmm == 23),
                                        inc=("st", 1) if nmm == 23 else None)
                            nmm += 1
                    col = dqc * 1024 + tt * 512
                    _, qf_tok[g] = b.esplit(ps_S[:, g * 512:(g + 1) * 512],
                                            Qfh[:, col:col + 512], Qfl[:, col:col + 512],
                                            wait=[("st", st_qf)])

                kfm_wave(0)
                vtm_wave(0)
                kfm_wave(1)
                vtm_wave(1)
                for g in range(4):
                    qfm_group(g)
                v_qf = qf_tok[3]
                v_v = vv_tok[1]

                NQ = 8
                psets = (ps_S, ps_T)
                I = [dict() for _ in range(NQ)]

                def emit_S_kt(k, kt, last):
                    d = I[k]
                    pset = psets[k % 2]
                    if k == 0:
                        w8 = [("sv", qf_tok[kt])]
                    elif k == 1:
                        w8 = [("sv", vv_tok[1])] if kt == 0 else ()
                    else:
                        key = {3: "pt2", 0: "v_o", 1: "stt", 2: None}[kt]
                        w8 = [("sv", I[k - 2][key])] if key else []
                    nmm = 0
                    st_sc = None
                    for dqc in range(2):
                        qcol = dqc * 1024 + k * 128
                        kcol = dqc * T + kt * 512
                        for (qt, kt_t) in ((Qfh, Kfh), (Qfh, Kfl), (Qfl, Kfh)):
                            st_sc = b.q("tensor", "matmul",
                                        pset[:, kt * 512:(kt + 1) * 512],
                                        qt[:, qcol:qcol + 128],
                                        kt_t[:, kcol:kcol + 512],
                                        wait=w8 if nmm == 0 else (),
                                        start=(nmm == 0), stop=(nmm == 5),
                                        inc=("st", 1) if (last and nmm == 5) else None)
                            nmm += 1
                    if last:
                        d["st_sc"] = st_sc

                def emit_smax(k):
                    d = I[k]
                    pset = psets[k % 2]
                    par = (k % 2) * 2
                    w8 = [("st", d["st_sc"])]
                    if k >= 1:
                        w8 += [("st", I[k - 1]["st_tpB"]), ("sv", I[k - 1]["v_sum"])]
                    ss_exp = b.q("scalar", "activation", Pb[:, :], pset[:, :], AF.Exp,
                                 bias=cols[:, 7:8], scale=SCALE, wait=w8)
                    d["ss_exp"] = ss_exp
                    d["v_sum"] = b.q("vector", "reduce_sum", sums[:, par:par + 1], Pb[:, :],
                                     axis=AX.X, wait=[("ss", ss_exp)])
                    d["v_rq"] = b.q("vector", "reciprocal", sums[:, par + 1:par + 2],
                                    sums[:, par:par + 1])

                def emit_B_pre(k):
                    d = I[k]
                    pset = psets[k % 2]
                    st_tp = {}
                    for kc in range(16):
                        w8 = [("ss", d["ss_exp"])] if kc == 0 else ()
                        tgt = b.q("tensor", "transpose", pset[:, kc * 128:(kc + 1) * 128],
                                  Pb[:, kc * 128:(kc + 1) * 128], idn[:, :], wait=w8,
                                  inc=("st", 1) if kc % 4 == 3 else None)
                        if kc % 4 == 3:
                            st_tp[kc // 4] = tgt
                    d["st_tpB"] = st_tp[3]
                    pt_sv = {}
                    for bk in range(4):
                        _, pt_sv[bk] = b.esplit(pset[:, bk * 512:(bk + 1) * 512],
                                                PTh[:, bk * 512:(bk + 1) * 512],
                                                PTl[:, bk * 512:(bk + 1) * 512],
                                                wait=[("st", st_tp[bk])])
                    d["pt2"] = pt_sv[3]
                    st_av = None
                    nmm = 0
                    for dc in range(2):
                        for kc in range(16):
                            for (vt_, pt_) in ((Vth, PTh), (Vth, PTl), (Vtl, PTh)):
                                w8 = ()
                                if dc == 0 and kc % 4 == 0 and nmm % 3 == 0:
                                    w8 = [("sv", pt_sv[kc // 4])]
                                st_av = b.q("tensor", "matmul",
                                            pset[:, dc * 128:(dc + 1) * 128],
                                            vt_[:, kc * DH + dc * 128: kc * DH + dc * 128 + 128],
                                            pt_[:, kc * 128:(kc + 1) * 128], wait=w8,
                                            start=(kc == 0 and nmm % 3 == 0),
                                            stop=(kc == 15 and nmm % 3 == 2),
                                            inc=("st", 1) if (dc, kc, nmm % 3) == (1, 15, 2) else None)
                                nmm += 1
                    _, v_o = b.esplit(pset[:, 0:DH], osh[:, 0:DH], osl[:, 0:DH],
                                      wait=[("st", st_av)])
                    d["v_o"] = v_o

                def emit_B_post(k):
                    d = I[k]
                    pset = psets[k % 2]
                    til = k
                    st_wo = None
                    for dmh in range(2):
                        nmm = 0
                        for dc in range(2):
                            for (ot, wt) in ((osh, woTh), (osh, woTl), (osl, woTh)):
                                w8 = [("sv", d["v_o"])] if (dmh, nmm) == (0, 0) else ()
                                if (k, dmh, nmm) == (0, 0, 0):
                                    w8 = list(w8) + [t_woh, t_wol]
                                st_wo = b.q("tensor", "matmul",
                                            pset[:, 512 + dmh * 512: 512 + (dmh + 1) * 512],
                                            ot[:, dc * 128:(dc + 1) * 128],
                                            wt[:, dc * D + dmh * 512: dc * D + (dmh + 1) * 512],
                                            wait=w8, start=(nmm == 0), stop=(nmm == 5),
                                            inc=("st", 1) if (dmh, nmm) == (1, 5) else None)
                                nmm += 1
                    par = (k % 2) * 2
                    v_stt = None
                    for dmh in range(2):
                        w8 = []
                        if dmh == 0:
                            w8 = [("st", st_wo), ("sv", d["v_rq"]), acc1_tok]
                        v_stt = b.q("vector", "scalar_tensor_tensor",
                                    acc[:, til * D + dmh * 512: til * D + (dmh + 1) * 512],
                                    pset[:, 512 + dmh * 512: 512 + (dmh + 1) * 512],
                                    sums[:, par + 1:par + 2],
                                    acc[:, til * D + dmh * 512: til * D + (dmh + 1) * 512],
                                    OP.mult, OP.add, wait=w8, inc=("sv", 1))
                    d["stt"] = v_stt

                for slot in range(NQ + 2):
                    k, kb = slot, slot - 2
                    if kb >= 0:
                        emit_B_pre(kb)
                    if k < NQ and k >= 2:
                        emit_S_kt(k, 3, last=False)
                    if kb >= 0:
                        emit_B_post(kb)
                    if 1 <= slot <= NQ:
                        emit_smax(slot - 1)
                    if k < NQ:
                        if k >= 2:
                            for kt in (0, 1, 2):
                                emit_S_kt(k, kt, last=(kt == 2))
                        else:
                            for kt in (0, 1, 2, 3):
                                emit_S_kt(k, kt, last=(kt == 3))
                b.flush(f"AT{s}{half}h{h}")
            b.off = False
            asc.close()

            fsc = contextlib.ExitStack()
            ffnwh = b.sbuf(f"ffnwh{s}{half}", [128, 8192], scope=fsc, dtype=F16)
            ffnwl = b.sbuf(f"ffnwl{s}{half}", [128, 8192], scope=fsc, dtype=F16)
            stg32 = b.sbuf(f"stg32{s}{half}", [128, 4 * 1024], scope=fsc)
            hqbh = b.sbuf(f"hqbh{s}{half}", [128, 4 * 1024], scope=fsc, dtype=F16)
            hqbl = b.sbuf(f"hqbl{s}{half}", [128, 4 * 1024], scope=fsc, dtype=F16)
            tmp5 = b.sbuf(f"tmp5{s}{half}", [128, 512], scope=fsc)
            ones_row = b.sbuf(f"ones{s}{half}", [1, 128], scope=fsc, dtype=F16)
            b2rh = b.sbuf(f"b2rh{s}{half}", [1, D], scope=fsc, dtype=F16)
            b2rl = b.sbuf(f"b2rl{s}{half}", [1, D], scope=fsc, dtype=F16)
            v_ones = [0]
            tb2 = [("d0", 0), ("d0", 0)]
            ps_f1 = b.psum(f"psf1{s}{half}", [128, 2048], scope=fsc)
            ps_f2 = b.psum(f"psf2{s}{half}", [128, 2048], scope=fsc)
            ps_of = (ps_f1, ps_f2)

            pf = {}
            mjobs = moe_jobs(half)
            bank_rel = {bi: ("d0", 0) for bi in range(8)}
            stg_rd = {bi: ("d0", 0) for bi in range(8)}
            hqb_rd = {bi: ("d0", 0) for bi in range(8)}
            bias_rd = [("d0", 0)]
            b.new_phase()
            if pending_out[0] is not None:
                emit_out(*pending_out[0])
                acc1_tok = acc1_box[0]
            if "ffnsh" in stages:
              for hq in range(4):
                if hq == 0:
                    tb2[0] = b.dma("sync", out=b2rh[0:1, :],
                                   in_=bass.AP(sb2h_d, 0, [[1, 1], [1, D]]))
                    tb2[1] = b.dma("sync", out=b2rl[0:1, :],
                                   in_=bass.AP(sb2l_d, 0, [[1, 1], [1, D]]))
                    v_ones[0] = b.q("vector", "memset", ones_row[:, :], 1.0)

                def zmm(tw, vprev):
                    close = {}
                    chunk_close = [0] * 8
                    for c in range(8):
                        for tt in range(2):
                            psf = ps_of[tt]
                            for j in range(4):
                                lhs_col = c * 512 + j * 128
                                rhs_col = c * T + hbase + tt * 512
                                nmm = 0
                                for (wt, xt_) in ((ffnwh, xnh), (ffnwh, xnl), (ffnwl, xnh)):
                                    w8 = []
                                    if (tt, j, nmm) == (0, 0, 0):
                                        w8.extend(tw[c])
                                    if c == 0 and nmm == 0 and vprev is not None:
                                        w8.append(vprev[(tt, j)])
                                    inc = None
                                    if (c == 7 or (tt, j) == (1, 3)) and nmm == 2:
                                        inc = ("st", 1)
                                    tgt = b.q("tensor", "matmul", psf[:, j * 512:(j + 1) * 512],
                                              wt[:, lhs_col:lhs_col + 128],
                                              xt_[:, rhs_col:rhs_col + 512],
                                              wait=w8, start=(c == 0 and nmm == 0),
                                              stop=(c == 7 and nmm == 2), inc=inc)
                                    nmm += 1
                                if c == 7:
                                    close[(tt, j)] = tgt
                                if (tt, j) == (1, 3):
                                    chunk_close[c] = tgt
                    return close, chunk_close

                def wload(cols_out, src_h, src_l, wait=()):
                    th = b.dma("sync", out=ffnwh[:, cols_out], in_=src_h, wait=wait)
                    tl = b.dma("sync", out=ffnwl[:, cols_out], in_=src_l, wait=wait)
                    return [th, tl]

                tw1 = [pf.pop(c, None) or
                       wload(slice(c * 512, (c + 1) * 512),
                             sw1h_d.ap()[c * 128:(c + 1) * 128, hq * 512:(hq + 1) * 512],
                             sw1l_d.ap()[c * 128:(c + 1) * 128, hq * 512:(hq + 1) * 512])
                       for c in range(8)]
                close1, cc1 = zmm(tw1, {(tt, j): bank_rel[tt * 4 + j]
                                        for tt in range(2) for j in range(4)}
                                  if hq > 0 else None)
                v_z1e = {}
                for tt in range(2):
                    psf = ps_of[tt]
                    for j in range(4):
                        bcol = sb1c[:, hq * 4 + j: hq * 4 + j + 1]
                        v_z1e[(tt, j)] = b.q(
                            "scalar", "activation",
                            stg32[:, j * 1024 + tt * 512: j * 1024 + (tt + 1) * 512],
                            psf[:, j * 512:(j + 1) * 512],
                            AF.Silu, bias=bcol,
                            wait=[("st", close1[(tt, j)]), stg_rd[2 * j + tt]],
                            inc=("ss", 1))
                tw3 = [wload(slice(c * 512, (c + 1) * 512),
                             sw3h_d.ap()[c * 128:(c + 1) * 128, hq * 512:(hq + 1) * 512],
                             sw3l_d.ap()[c * 128:(c + 1) * 128, hq * 512:(hq + 1) * 512],
                             wait=[("st", cc1[c])])
                       for c in range(8)]
                close3, cc3 = zmm(tw3, {k: ("ss", v) for k, v in v_z1e.items()})
                v_z3 = 0
                hq_e = {}
                for j in range(4):
                    for tt in range(2):
                        psf = ps_of[tt]
                        hcol = j * 1024 + tt * 512
                        bcol = sb3c[:, hq * 4 + j: hq * 4 + j + 1]
                        vts3 = b.q("vector", "tensor_scalar", tmp5[:, :], psf[:, j * 512:(j + 1) * 512],
                                   bcol, None, OP.add,
                                   wait=[("st", close3[(tt, j)]), ("sv", v_z3)], inc=("sv", 1))
                        v_z3 = b.q("vector", "tensor_tensor",
                                   stg32[:, hcol:hcol + 512],
                                   stg32[:, hcol:hcol + 512],
                                   tmp5[:, :], OP.mult, wait=[("sv", vts3)], inc=("sv", 1))
                        ss_hh = b.q("scalar", "copy", hqbh[:, hcol:hcol + 512],
                                    stg32[:, hcol:hcol + 512],
                                    wait=[("sv", v_z3), hqb_rd[2 * j + tt]])
                        v_z3 = b.q("vector", "tensor_tensor", hqbl[:, hcol:hcol + 512],
                                   stg32[:, hcol:hcol + 512], hqbh[:, hcol:hcol + 512],
                                   OP.subtract, wait=[("ss", ss_hh)])
                        hq_e[(tt, j)] = v_z3
                tw2 = [wload(slice(jj * 1024, (jj + 1) * 1024),
                             sw2h_d.ap()[hq * 512 + jj * 128: hq * 512 + (jj + 1) * 128, :],
                             sw2l_d.ap()[hq * 512 + jj * 128: hq * 512 + (jj + 1) * 128, :],
                             wait=[("st", cc3[2 * jj + 1])])
                       for jj in range(4)]
                groups = [(tt16, ch) for tt16 in range(8) for ch in range(2)]
                stt_tgt = [0] * 8
                fs_jj_tok = [0] * 4
                for w in range(2):
                    wave = groups[w * 8:(w + 1) * 8]
                    close_w = [None] * 8
                    for jj in range(4):
                        for gi, (tt16, ch) in enumerate(wave):
                            psf = ps_of[gi // 4]
                            pslice = psf[:, (gi % 4) * 512:((gi % 4) + 1) * 512]
                            lhs_col = jj * 1024 + tt16 * 128
                            rhs_col = jj * 1024 + ch * 512
                            nmm = 0
                            for (ht, wt) in ((hqbh, ffnwh), (hqbh, ffnwl), (hqbl, ffnwh)):
                                w8 = []
                                if gi == 0 and nmm == 0:
                                    w8.extend(tw2[jj])
                                    if w == 0:
                                        w8.append(("sv", max(hq_e[(0, jj)], hq_e[(1, jj)])))
                                if jj == 0 and nmm == 0:
                                    if w == 0:
                                        w8.append(("sv", hq_e[(gi // 4, gi % 4)]))
                                    if w == 1:
                                        w8.append(("sv", stt_tgt[gi]))
                                last = (jj == 3 and nmm == 2 and hq != 0)
                                incw = ("st", 1) if (last or (w == 1 and gi == 7 and nmm == 2)) else None
                                close_w[gi] = b.q("tensor", "matmul", pslice,
                                                  ht[:, lhs_col:lhs_col + 128],
                                                  wt[:, rhs_col:rhs_col + 512],
                                                  wait=w8, start=(jj == 0 and nmm == 0),
                                                  stop=last, inc=incw)
                                nmm += 1
                            if w == 1 and gi == 7:
                                fs_jj_tok[jj] = b.ct["st"]
                    if hq == 0:
                        for gi, (tt16, ch) in enumerate(wave):
                            psf = ps_of[gi // 4]
                            pslice = psf[:, (gi % 4) * 512:((gi % 4) + 1) * 512]
                            b.q("tensor", "matmul", pslice, ones_row[0:1, :],
                                b2rh[0:1, ch * 512:(ch + 1) * 512],
                                wait=[("sv", v_ones[0]), tb2[0], tb2[1]] if (w, gi) == (0, 0) else (),
                                start=False, stop=False)
                            close_w[gi] = b.q("tensor", "matmul", pslice, ones_row[0:1, :],
                                              b2rl[0:1, ch * 512:(ch + 1) * 512],
                                              start=False, stop=True, inc=("st", 1))
                    for gi, (tt16, ch) in enumerate(wave):
                        psf = ps_of[gi // 4]
                        pslice = psf[:, (gi % 4) * 512:((gi % 4) + 1) * 512]
                        stt_tgt[gi] = b.q("vector", "scalar_tensor_tensor",
                                          acc[:, tt16 * D + ch * 512: tt16 * D + (ch + 1) * 512],
                                          pslice, 0.5,
                                          acc[:, tt16 * D + ch * 512: tt16 * D + (ch + 1) * 512],
                                          OP.mult, OP.add,
                                          wait=[("st", close_w[gi]), acc1_tok], inc=("sv", 1))
                if hq == 0:
                    bias_rd[0] = ("st", b.ct["st"])
                for bi in range(8):
                    bank_rel[bi] = ("sv", stt_tgt[bi])
                    stg_rd[bi] = ("sv", hq_e[(bi % 2, bi // 2)])
                    hqb_rd[bi] = ("st", fs_jj_tok[bi // 2])
                if hq < 3:
                    for c in range(8):
                        pf[c] = wload(slice(c * 512, (c + 1) * 512),
                                      sw1h_d.ap()[c * 128:(c + 1) * 128, (hq + 1) * 512:(hq + 2) * 512],
                                      sw1l_d.ap()[c * 128:(c + 1) * 128, (hq + 1) * 512:(hq + 2) * 512],
                                      wait=[("st", fs_jj_tok[c // 2])])
                elif "moe" in stages and mjobs:
                    e0 = mjobs[0][0]
                    for c in range(8):
                        pf[c] = wload(slice(c * 1024, (c + 1) * 1024),
                                      w1h_d.ap()[e0, c * 128:(c + 1) * 128, :],
                                      w1l_d.ap()[e0, c * 128:(c + 1) * 128, :],
                                      wait=[("st", fs_jj_tok[c])] if c < 4 else ())

            for ji, (e, rs) in enumerate(mjobs if "moe" in stages else ()):
                ngrp = len(rs)
                if not v_ones[0]:
                    v_ones[0] = b.q("vector", "memset", ones_row[:, :], 1.0)
                tw1m = [pf.pop(c, None) or
                        wload(slice(c * 1024, (c + 1) * 1024),
                              w1h_d.ap()[e, c * 128:(c + 1) * 128, :],
                              w1l_d.ap()[e, c * 128:(c + 1) * 128, :])
                        for c in range(8)]
                close_j = [None] * 8
                chunk_close = [0] * 8
                for c in range(8):
                    for j in range(8):
                        psf = ps_of[j // 4]
                        for g, r in enumerate(rs):
                            rcol = (r - half * 4) * 256
                            nmm = 0
                            for (wt, xt_) in ((ffnwh, xnh), (ffnwh, xnl), (ffnwl, xnh)):
                                w8 = []
                                if (j, g, nmm) == (0, 0, 0):
                                    w8.extend(tw1m[c])
                                if (c, g, nmm) == (0, 0, 0):
                                    w8.append(bank_rel[j])
                                inc = None
                                if c == 7 and g == ngrp - 1 and nmm == 2:
                                    inc = ("st", 1)
                                if (j, g, nmm) == (7, ngrp - 1, 2):
                                    inc = ("st", 1)
                                tgt = b.q("tensor", "matmul",
                                          psf[:, (j % 4) * 512 + g * 256: (j % 4) * 512 + (g + 1) * 256],
                                          wt[:, c * 1024 + j * 128: c * 1024 + j * 128 + 128],
                                          xt_[:, c * T + hbase + rcol: c * T + hbase + rcol + 256],
                                          wait=w8, start=(c == 0 and g == 0 and nmm == 0),
                                          stop=(c == 7 and g == ngrp - 1 and nmm == 2), inc=inc)
                                nmm += 1
                            if c == 7 and g == ngrp - 1:
                                close_j[j] = tgt
                            if (j, g) == (7, ngrp - 1):
                                chunk_close[c] = tgt
                hm_e = [0] * 8
                ss_gj = [0] * 8
                for j in range(8):
                    psf = ps_of[j // 4]
                    w_j = j * 512
                    n_j = ngrp * 256
                    ss_gj[j] = b.q("scalar", "activation",
                                   stg32[:, w_j: w_j + n_j],
                                   psf[:, (j % 4) * 512: (j % 4) * 512 + n_j],
                                   AF.Gelu, bias=b1c[:, e * 8 + j: e * 8 + j + 1],
                                   wait=[("st", close_j[j]), stg_rd[j]], inc=("ss", 1))
                    v_hh = b.q("vector", "tensor_copy", hqbh[:, w_j: w_j + n_j],
                               stg32[:, w_j: w_j + n_j],
                               wait=[("ss", ss_gj[j]), hqb_rd[j]], inc=("sv", 1))
                    hm_e[j] = b.q("vector", "tensor_tensor", hqbl[:, w_j: w_j + n_j],
                                  stg32[:, w_j: w_j + n_j], hqbh[:, w_j: w_j + n_j],
                                  OP.subtract, wait=[("sv", v_hh)], inc=("sv", 1))
                tw2m = [wload(slice(c * 1024, (c + 1) * 1024),
                              w2h_d.ap()[e, c * 128:(c + 1) * 128, :],
                              w2l_d.ap()[e, c * 128:(c + 1) * 128, :],
                              wait=[("st", chunk_close[c])])
                        for c in range(8)]
                t_b2h = b.dma("sync", out=b2rh[0:1, :],
                              in_=bass.AP(b2h_d, e * D, [[1, 1], [1, D]]),
                              wait=[bias_rd[0]])
                t_b2l = b.dma("sync", out=b2rl[0:1, :],
                              in_=bass.AP(b2l_d, e * D, [[1, 1], [1, D]]))
                wgroups = [(g, sub, ch) for g in range(ngrp) for sub in range(2) for ch in range(2)]
                close_w = [None] * len(wgroups)
                mo_j_tok = [0] * 8
                for j in range(8):
                    for gi, (g, sub, ch) in enumerate(wgroups):
                        psf = ps_of[gi // 4]
                        pslice = psf[:, (gi % 4) * 512:((gi % 4) + 1) * 512]
                        lhs_col = j * 512 + g * 256 + sub * 128
                        rhs_col = j * 1024 + ch * 512
                        nmm = 0
                        for (ht, wt) in ((hqbh, ffnwh), (hqbh, ffnwl), (hqbl, ffnwh)):
                            w8 = []
                            if gi == 0 and nmm == 0:
                                w8.extend(tw2m[j])
                                w8.append(("sv", hm_e[j]))
                            if j == 0 and nmm == 0:
                                w8.append(("ss", ss_gj[gi]))
                            lastg = (gi == len(wgroups) - 1 and nmm == 2)
                            close_w[gi] = b.q("tensor", "matmul", pslice,
                                              ht[:, lhs_col:lhs_col + 128],
                                              wt[:, rhs_col:rhs_col + 512],
                                              wait=w8, start=(j == 0 and nmm == 0), stop=False,
                                              inc=("st", 1) if lastg else None)
                            nmm += 1
                        if gi == len(wgroups) - 1:
                            mo_j_tok[j] = b.ct["st"]
                for gi, (g, sub, ch) in enumerate(wgroups):
                    psf = ps_of[gi // 4]
                    pslice = psf[:, (gi % 4) * 512:((gi % 4) + 1) * 512]
                    b.q("tensor", "matmul", pslice, ones_row[0:1, :],
                        b2rh[0:1, ch * 512:(ch + 1) * 512],
                        wait=([t_b2h, t_b2l, ("sv", v_ones[0])] if gi == 0 else ()),
                        start=False, stop=False)
                    close_w[gi] = b.q("tensor", "matmul", pslice, ones_row[0:1, :],
                                      b2rl[0:1, ch * 512:(ch + 1) * 512],
                                      start=False, stop=True, inc=("st", 1))
                stt_moe = [0] * len(wgroups)
                for gi, (g, sub, ch) in enumerate(wgroups):
                    psf = ps_of[gi // 4]
                    pslice = psf[:, (gi % 4) * 512:((gi % 4) + 1) * 512]
                    til = (rs[g] - half * 4) * 2 + sub
                    stt_moe[gi] = b.q("vector", "scalar_tensor_tensor",
                                      acc[:, til * D + ch * 512: til * D + (ch + 1) * 512],
                                      pslice, 0.25,
                                      acc[:, til * D + ch * 512: til * D + (ch + 1) * 512],
                                      OP.mult, OP.add,
                                      wait=[("st", close_w[gi]), acc1_tok], inc=("sv", 1))
                bias_rd[0] = ("st", b.ct["st"])
                for bi in range(8):
                    bank_rel[bi] = (("sv", stt_moe[bi]) if bi < len(wgroups)
                                    else ("ss", ss_gj[bi]))
                    stg_rd[bi] = ("sv", hm_e[bi])
                    hqb_rd[bi] = ("st", mo_j_tok[bi])
                if ji + 1 < len(mjobs):
                    e_nx = mjobs[ji + 1][0]
                    for c in range(8):
                        pf[c] = wload(slice(c * 1024, (c + 1) * 1024),
                                      w1h_d.ap()[e_nx, c * 128:(c + 1) * 128, :],
                                      w1l_d.ap()[e_nx, c * 128:(c + 1) * 128, :],
                                      wait=[("st", mo_j_tok[c])])
            b.flush(f"FFN{s}{half}")
            fsc.close()

            pending_out[0] = (s, half)
            if (s, half) == (NSEQ - 1, 1):
                b.new_phase()
                emit_out(s, half)
                b.qwait("sync", [("so", out_total[0])])
                b.flush(f"OUT{s}{half}")
    return b


_nc_cache = None


def _get_nc():
    global _nc_cache
    if _nc_cache is None:
        _nc_cache = build_nc().nc
    return _nc_cache


def _perm():
    j = np.arange(T)
    t_of_j = (j % 256) * 8 + (j // 256)
    inv = np.empty(T, np.int64)
    inv[t_of_j] = j
    return t_of_j, inv


def kernel(**inputs):
    x = np.asarray(inputs["L0_attn_x"], dtype=np.float32).reshape(B, T, D)
    t_of_j, inv = _perm()
    weights = {
        "Wq": inputs["L0_attn_Wq"], "Wkv": inputs["L0_attn_Wkv_down"],
        "Wku": inputs["L0_attn_Wk_up"], "Wvu": inputs["L0_attn_Wv_up"],
        "Wo": inputs["L0_attn_Wo"],
        "sw1": inputs["L0_ffn_sw1"], "sb1": inputs["L0_ffn_sb1"],
        "sw3": inputs["L0_ffn_sw3"], "sb3": inputs["L0_ffn_sb3"],
        "sw2": inputs["L0_ffn_sw2"], "sb2": inputs["L0_ffn_sb2"],
        "w1": inputs["L0_ffn_w1"], "b1": inputs["L0_ffn_b1"],
        "w2": inputs["L0_ffn_w2"], "b2": inputs["L0_ffn_b2"],
    }
    weights = {k: np.ascontiguousarray(np.asarray(v, dtype=np.float32)) for k, v in weights.items()}
    ident = np.eye(128, dtype=np.float32)
    nc = _get_nc()
    in_maps = []
    for c in range(8):
        xc = np.ascontiguousarray(x[2 * c:2 * c + 2][:, t_of_j, :].reshape(NTOK, D))
        im = {"x": xc, "ident": ident}
        im.update(weights)
        in_maps.append(im)
    res = run_bass_kernel_spmd(nc, in_maps, core_ids=list(range(8)))
    out = np.empty((B, T, D), dtype=np.float32)
    for c in range(8):
        yc = np.asarray(res.results[c]["y"]).reshape(NSEQ, T, D)
        out[2 * c:2 * c + 2] = yc[:, inv, :]
    return out


# version 26
# speedup vs baseline: 1.0481x; 1.0481x over previous
import contextlib
import numpy as np
import concourse.bass as bass
import concourse.mybir as mybir
from concourse.bass_utils import run_bass_kernel_spmd

F32 = mybir.dt.float32
F16 = mybir.dt.float16
AF = mybir.ActivationFunctionType
OP = mybir.AluOpType
AX = mybir.AxisListType

B, T, D = 16, 2048, 1024
H, DH, LR = 8, 256, 64
HS, E, HR = 2048, 8, 1024
NSEQ = 2
NTOK = NSEQ * T
SCALE = 1.0 / 16.0

ENGS = ("sync", "vector", "scalar", "tensor", "gpsimd")
NDMA_SEM = 8


class Slab:

    def __init__(self, h, base, size=None):
        self.h, self.base, self.size = h, base, size

    def __getitem__(self, idx):
        r, c = idx
        stop = c.stop if c.stop is not None else self.size
        return self.h[r, (self.base + (c.start or 0)):(self.base + stop)]


class Bld:
    def __init__(self):
        self.nc = bass.Bass("TRN2", target_bir_lowering=False)
        self.es = contextlib.ExitStack()
        self.ct = {}
        self.sems = {}
        self.ph = None
        self.off = False
        self.didx = 0
        self.gidx = 0
        self.last = {"vector": 0, "scalar": 0}
        self.es_hist = []

    def sem(self, name):
        h = self.es.enter_context(self.nc.semaphore(name))
        self.sems[name] = h
        self.ct[name] = 0

    def sbuf(self, name, shape, scope=None, dtype=F32):
        return (scope or self.es).enter_context(self.nc.sbuf_tensor(name, shape, dtype))

    def psum(self, name, shape, scope=None):
        return (scope or self.es).enter_context(self.nc.psum_tensor(name, shape, F32))

    def new_phase(self):
        self.ph = {k: [] for k in ENGS}

    def q(self, eng, method, *args, wait=(), inc=None, **kw):
        if self.off:
            return self.ct[inc[0]] if inc else None
        wait = list(wait)
        if eng == "vector":
            wait.append(("sv", self.last["vector"]))
            if inc is None:
                inc = ("sv", 1)
        elif eng == "scalar":
            wait.append(("ss", self.last["scalar"]))
            if inc is None:
                inc = ("ss", 1)
        wl = [(self.sems[s], v) for (s, v) in wait if v > 0]
        tgt = None
        incspec = None
        if inc is not None:
            s, amt = inc
            self.ct[s] += amt
            tgt = self.ct[s]
            incspec = (self.sems[s], amt)
        if eng == "vector":
            self.last["vector"] = tgt
        elif eng == "scalar":
            self.last["scalar"] = tgt

        def fn(Eng, method=method, args=args, kw=kw, wl=wl, incspec=incspec):
            for (h, v) in wl:
                Eng.wait_ge(h, v)
            inst = getattr(Eng, method)(*args, **kw)
            if incspec is not None:
                inst.then_inc(*incspec)

        self.ph[eng].append(fn)
        return tgt

    def dma(self, eng, out, in_, wait=(), **kw):
        if self.off:
            return ("d0", 0)
        if eng == "gpsimd":
            name = f"g{self.gidx % 8}"
            self.gidx += 1
        else:
            name = f"d{self.didx % NDMA_SEM}"
            self.didx += 1
        wl = list(wait)
        if self.ct[name] > 0:
            wl.append((name, self.ct[name]))
        tgt = self.q(eng, "dma_start", out=out, in_=in_, wait=wl, inc=(name, 16), **kw)
        return (name, tgt)

    def esplit(self, ps_slice, hi_ap, lo_ap, wait=()):
        wait = list(wait)
        if len(self.es_hist) >= 2:
            wait.append(("sv", self.es_hist[-2]))
        ss_t = self.q("scalar", "copy", hi_ap, ps_slice, wait=wait)
        sv_t = self.q("vector", "tensor_tensor", lo_ap, ps_slice, hi_ap,
                      OP.subtract, wait=[("ss", ss_t)])
        self.es_hist.append(sv_t)
        return ss_t, sv_t

    def qwait(self, eng, wait):
        if self.off:
            return
        wl = [(self.sems[s], v) for (s, v) in wait if v > 0]

        def fn(Eng, wl=wl):
            for (h, v) in wl:
                Eng.wait_ge(h, v)

        self.ph[eng].append(fn)

    def flush(self, name):
        ph = self.ph
        self.qwait("vector", [("sv", self.ct["sv"])])
        self.qwait("scalar", [("ss", self.ct["ss"])])
        with self.nc.Block(name) as blk:
            @blk.sync
            def _(Eng):
                for f in ph["sync"]:
                    f(Eng)

            @blk.vector
            def _(Eng):
                for f in ph["vector"]:
                    f(Eng)

            @blk.scalar
            def _(Eng):
                for f in ph["scalar"]:
                    f(Eng)

            @blk.tensor
            def _(Eng):
                for f in ph["tensor"]:
                    f(Eng)

            @blk.gpsimd
            def _(Eng):
                for f in ph["gpsimd"]:
                    f(Eng)
        self.ph = None


def moe_jobs(half):
    jobs = {}
    for r in range(half * 4, half * 4 + 4):
        for e in ((3 * r) % 8, (3 * r + 7) % 8):
            jobs.setdefault(e, []).append(r)
    return sorted(jobs.items())


def build_nc(stages=("att", "ffnsh", "moe")):
    b = Bld()
    nc = b.nc

    xd = nc.dram_tensor("x", [NTOK, D], F32, kind="ExternalInput")
    yd = nc.dram_tensor("y", [NTOK, D], F32, kind="ExternalOutput")
    identd = nc.dram_tensor("ident", [128, 128], F32, kind="ExternalInput")
    wq_d = nc.dram_tensor("Wq", [H * DH, D], F32, kind="ExternalInput")
    wkv_d = nc.dram_tensor("Wkv", [LR, D], F32, kind="ExternalInput")
    wku_d = nc.dram_tensor("Wku", [H * DH, LR], F32, kind="ExternalInput")
    wvu_d = nc.dram_tensor("Wvu", [H * DH, LR], F32, kind="ExternalInput")
    wo_d = nc.dram_tensor("Wo", [D, H * DH], F32, kind="ExternalInput")
    sw1_d = nc.dram_tensor("sw1", [D, HS], F32, kind="ExternalInput")
    sb1_d = nc.dram_tensor("sb1", [HS], F32, kind="ExternalInput")
    sw3_d = nc.dram_tensor("sw3", [D, HS], F32, kind="ExternalInput")
    sb3_d = nc.dram_tensor("sb3", [HS], F32, kind="ExternalInput")
    sw2_d = nc.dram_tensor("sw2", [HS, D], F32, kind="ExternalInput")
    sb2_d = nc.dram_tensor("sb2", [D], F32, kind="ExternalInput")
    w1_d = nc.dram_tensor("w1", [E, D, HR], F32, kind="ExternalInput")
    b1_d = nc.dram_tensor("b1", [E, HR], F32, kind="ExternalInput")
    w2_d = nc.dram_tensor("w2", [E, HR, D], F32, kind="ExternalInput")
    b2_d = nc.dram_tensor("b2", [E, D], F32, kind="ExternalInput")
    wqTh_d = nc.dram_tensor("wqTh_s", [H * 128, 2 * D], F16, kind="Internal")
    wqTl_d = nc.dram_tensor("wqTl_s", [H * 128, 2 * D], F16, kind="Internal")
    woTh_d = nc.dram_tensor("woTh_s", [H * 128, 2 * D], F16, kind="Internal")
    woTl_d = nc.dram_tensor("woTl_s", [H * 128, 2 * D], F16, kind="Internal")
    sw1h_d = nc.dram_tensor("sw1h_s", [D, HS], F16, kind="Internal")
    sw1l_d = nc.dram_tensor("sw1l_s", [D, HS], F16, kind="Internal")
    sw3h_d = nc.dram_tensor("sw3h_s", [D, HS], F16, kind="Internal")
    sw3l_d = nc.dram_tensor("sw3l_s", [D, HS], F16, kind="Internal")
    sw2h_d = nc.dram_tensor("sw2h_s", [HS, D], F16, kind="Internal")
    sw2l_d = nc.dram_tensor("sw2l_s", [HS, D], F16, kind="Internal")
    w1h_d = nc.dram_tensor("w1h_s", [E, D, HR], F16, kind="Internal")
    w1l_d = nc.dram_tensor("w1l_s", [E, D, HR], F16, kind="Internal")
    w2h_d = nc.dram_tensor("w2h_s", [E, HR, D], F16, kind="Internal")
    w2l_d = nc.dram_tensor("w2l_s", [E, HR, D], F16, kind="Internal")
    b2h_d = nc.dram_tensor("b2h_s", [E, D], F16, kind="Internal")
    b2l_d = nc.dram_tensor("b2l_s", [E, D], F16, kind="Internal")
    sb2h_d = nc.dram_tensor("sb2h_s", [D], F16, kind="Internal")
    sb2l_d = nc.dram_tensor("sb2l_s", [D], F16, kind="Internal")

    x = xd.ap()
    ident = identd.ap()
    wkv = wkv_d.ap()

    for s in ("so", "sv", "ss", "st"):
        b.sem(s)
    for i in range(NDMA_SEM):
        b.sem(f"d{i}")
    for i in range(8):
        b.sem(f"g{i}")

    xnh = b.sbuf("xnh", [128, 8 * T], dtype=F16)
    xnl = b.sbuf("xnl", [128, 8 * T], dtype=F16)
    acc = b.sbuf("acc", [128, 8 * D])
    kvh = b.sbuf("kvh", [LR, T], dtype=F16)
    kvl = b.sbuf("kvl", [LR, T], dtype=F16)
    idn = b.sbuf("idn", [128, 128])
    lnx = b.sbuf("lnx", [128, D])
    xntmp = b.sbuf("xntmp", [128, D])
    misc = b.sbuf("misc", [128, 128])
    cols = Slab(misc, 0, 8)
    sums = Slab(misc, 8, 4)
    sb1c = Slab(misc, 16, 16)
    sb3c = Slab(misc, 32, 16)
    b1c = Slab(misc, 48, 64)
    lncols = Slab(misc, 112, 16)
    wkvT2 = b.sbuf("wkvT2", [128, 16 * LR], dtype=F16)
    wkvTh = Slab(wkvT2, 0, 8 * LR)
    wkvTl = Slab(wkvT2, 8 * LR, 8 * LR)
    wkuTh = b.sbuf("wkuTh", [LR, H * DH], dtype=F16)
    wkuTl = b.sbuf("wkuTl", [LR, H * DH], dtype=F16)
    wvuTh = b.sbuf("wvuTh", [LR, H * DH], dtype=F16)
    wvuTl = b.sbuf("wvuTl", [LR, H * DH], dtype=F16)

    wfsc = contextlib.ExitStack()
    NWF = 4
    wfAs = [b.sbuf(f"wfA{i}", [128, 1024], scope=wfsc) for i in range(NWF)]
    wfHs = [b.sbuf(f"wfH{i}", [128, 1024], scope=wfsc, dtype=F16) for i in range(NWF)]
    wfLs = [b.sbuf(f"wfL{i}", [128, 1024], scope=wfsc, dtype=F16) for i in range(NWF)]
    wf_chunks = []
    for src_d, dsth_d, dstl_d, nrow in ((sw1_d, sw1h_d, sw1l_d, 8),
                                        (sw3_d, sw3h_d, sw3l_d, 8)):
        for c in range(nrow):
            for hh in range(2):
                wf_chunks.append((src_d.ap()[c * 128:(c + 1) * 128, hh * 1024:(hh + 1) * 1024],
                                  dsth_d.ap()[c * 128:(c + 1) * 128, hh * 1024:(hh + 1) * 1024],
                                  dstl_d.ap()[c * 128:(c + 1) * 128, hh * 1024:(hh + 1) * 1024]))
    for c in range(16):
        wf_chunks.append((sw2_d.ap()[c * 128:(c + 1) * 128, :],
                          sw2h_d.ap()[c * 128:(c + 1) * 128, :],
                          sw2l_d.ap()[c * 128:(c + 1) * 128, :]))
    wf_chunks.append((b2_d.ap()[:, :], b2h_d.ap()[:, :], b2l_d.ap()[:, :], 8))
    wf_chunks.append((bass.AP(sb2_d, 0, [[1, 1], [1, D]]),
                      bass.AP(sb2h_d, 0, [[1, 1], [1, D]]),
                      bass.AP(sb2l_d, 0, [[1, 1], [1, D]]), 1))
    for e in range(E):
        for c in range(8):
            wf_chunks.append((w1_d.ap()[e, c * 128:(c + 1) * 128, :],
                              w1h_d.ap()[e, c * 128:(c + 1) * 128, :],
                              w1l_d.ap()[e, c * 128:(c + 1) * 128, :]))
        for c in range(8):
            wf_chunks.append((w2_d.ap()[e, c * 128:(c + 1) * 128, :],
                              w2h_d.ap()[e, c * 128:(c + 1) * 128, :],
                              w2l_d.ap()[e, c * 128:(c + 1) * 128, :]))
    wf_state = {"i": 0, "tok": [{"h": ("d0", 0), "l": ("d0", 0),
                                 "ss": 0, "sv": 0} for _ in range(NWF)]}

    def wf_emit(n):
        for _ in range(n):
            i = wf_state["i"]
            if i >= len(wf_chunks):
                return
            wf_state["i"] += 1
            ck = wf_chunks[i]
            (src, dsth, dstl), nr = ck[:3], (ck[3] if len(ck) > 3 else 128)
            par = i % NWF
            tk = wf_state["tok"][par]
            wfA, wfH, wfL = wfAs[par], wfHs[par], wfLs[par]
            t_in = b.dma("sync", out=wfA[0:nr, :], in_=src,
                         wait=[w for w in (("ss", tk["ss"]), ("sv", tk["sv"])) if w[1] > 0])
            ss_t = b.q("scalar", "copy", wfH[0:nr, :], wfA[0:nr, :],
                       wait=[t_in, tk["h"]])
            sv_t = b.q("vector", "tensor_tensor", wfL[0:nr, :], wfA[0:nr, :],
                       wfH[0:nr, :], OP.subtract, wait=[("ss", ss_t), tk["l"]])
            th = b.dma("gpsimd", out=dsth, in_=wfH[0:nr, :], wait=[("ss", ss_t)])
            tl = b.dma("gpsimd", out=dstl, in_=wfL[0:nr, :], wait=[("sv", sv_t)])
            wf_state["tok"][par] = {"h": th, "l": tl, "ss": ss_t, "sv": sv_t}

    wsc = contextlib.ExitStack()
    ps_w2 = b.psum("ps_w2", [128, 2048], scope=wsc)
    stgAs = (b.sbuf("stgA0", [128, 2048], scope=wsc), b.sbuf("stgA1", [128, 2048], scope=wsc))
    stgHs = (b.sbuf("stgH0", [128, 2048], scope=wsc, dtype=F16),
             b.sbuf("stgH1", [128, 2048], scope=wsc, dtype=F16))
    stgLs = (b.sbuf("stgL0", [128, 2048], scope=wsc, dtype=F16),
             b.sbuf("stgL1", [128, 2048], scope=wsc, dtype=F16))
    wkvstg = b.sbuf("wkvstg", [LR, D], scope=wsc)
    b.new_phase()
    t_id = b.dma("sync", out=idn[:, :], in_=ident[:, :])
    t_kv = b.dma("sync", out=wkvstg[0:LR, :], in_=wkv[:, :])
    b.q("vector", "memset", cols[:, 7:8], 0.0)
    b.dma("sync", out=sb1c[:, :], in_=bass.AP(sb1_d, 0, [[1, 128], [128, 16]]),
          allow_slow_non_contiguous=True)
    b.dma("sync", out=sb3c[:, :], in_=bass.AP(sb3_d, 0, [[1, 128], [128, 16]]),
          allow_slow_non_contiguous=True)
    for e in range(E):
        b.dma("sync", out=b1c[:, e * 8:(e + 1) * 8],
              in_=bass.AP(b1_d, e * HR, [[1, 128], [128, 8]]),
              allow_slow_non_contiguous=True)
    b.q("vector", "memset", cols[:, 6:7], 1e-5)
    st_w = None
    for c in range(8):
        st_w = b.q("tensor", "transpose", ps_w2[:, c * 64:(c + 1) * 64],
                   wkvstg[0:LR, c * 128:(c + 1) * 128], idn[0:LR, 0:LR],
                   wait=[t_id, t_kv] if c == 0 else (), inc=("st", 1))
    ss_wkv, v_wkv = b.esplit(ps_w2[:, 0:512], wkvTh[:, :], wkvTl[:, :],
                             wait=[("st", st_w)])

    wstate = {"st_stgA": [0, 0], "t_out": [[("d0", 0)], [("d0", 0)]],
              "ev": [("sv", v_wkv)], "item": 0}

    def w_prep(load_in, npart, tiles, out_h, out_l, load_cols=2048, split=True):
        par = wstate["item"] % 2
        wstate["item"] += 1
        stgA = stgAs[par]
        psw = ps_w2
        t_ld = b.dma("sync", out=stgA[:, 0:load_cols], in_=load_in,
                     wait=[("st", wstate["st_stgA"][par])] if wstate["st_stgA"][par] else ())
        st_t = {}
        iw = 64 if npart == LR else 128
        for i, (icol, ocol) in enumerate(tiles):
            w8 = [t_ld] + wstate["ev"] if i == 0 else ()
            tgt = b.q("tensor", "transpose", psw[0:npart, i * 128:(i + 1) * 128],
                      stgA[:, icol:icol + iw], idn[:, :],
                      wait=w8, inc=("st", 1))
            st_t[i] = tgt
        wstate["st_stgA"][par] = st_t[len(tiles) - 1]
        nt = len(tiles)
        nbank = (nt + 3) // 4
        order = [4 * bk + k for k in range(4) for bk in range(nbank) if 4 * bk + k < nt]
        evs = []
        if split:
            stgH, stgL = stgHs[par], stgLs[par]
            for ii, i in enumerate(order):
                ocol = tiles[i][1]
                w8 = [("st", st_t[min((i // 4) * 4 + 3, nt - 1)])]
                if ii == 0:
                    w8 += wstate["t_out"][par]
                ss_t, sv_t = b.esplit(psw[0:npart, i * 128:(i + 1) * 128],
                                      stgH[0:npart, ocol:ocol + 128],
                                      stgL[0:npart, ocol:ocol + 128], wait=w8)
                evs.append(("sv", sv_t))
            ev2 = [("sv", max(v for _, v in evs))]
            wstate["ev"] = ev2
            th = b.dma("sync", out=out_h, in_=stgH[0:npart, :], wait=ev2)
            tl = b.dma("sync", out=out_l, in_=stgL[0:npart, :], wait=ev2)
            wstate["t_out"][par] = [th, tl]
        else:
            for ii, i in enumerate(order):
                ocol = tiles[i][1]
                w8 = [("st", st_t[min((i // 4) * 4 + 3, nt - 1)])]
                _, sv_t = b.esplit(psw[0:npart, i * 128:(i + 1) * 128],
                                   out_h[0:npart, ocol:ocol + 128],
                                   out_l[0:npart, ocol:ocol + 128], wait=w8)
                evs.append(("sv", sv_t))
            ev2 = [("sv", max(v for _, v in evs))]
            wstate["ev"] = ev2
            wstate["t_out"][par] = []

    for h in range(H):
        w_prep(bass.AP(wq_d, h * DH * D, [[1024, 128], [128 * 1024, 2], [1, 1024]]), 128,
               [(dqc * 1024 + c * 128, c * DH + dqc * 128) for dqc in range(2) for c in range(8)],
               wqTh_d.ap()[h * 128:(h + 1) * 128, :], wqTl_d.ap()[h * 128:(h + 1) * 128, :])
        w_prep(bass.AP(wo_d, h * DH, [[2048, 128], [128 * 2048, 8], [1, DH]]), 128,
               [(j * DH + dc * 128, dc * D + j * 128) for j in range(8) for dc in range(2)],
               woTh_d.ap()[h * 128:(h + 1) * 128, :], woTl_d.ap()[h * 128:(h + 1) * 128, :])
    w_prep(bass.AP(wku_d, 0, [[64, 128], [128 * 64, 16], [1, 64]]), LR,
           [(i * 64, i * 128) for i in range(16)], wkuTh, wkuTl,
           load_cols=1024, split=False)
    w_prep(bass.AP(wvu_d, 0, [[64, 128], [128 * 64, 16], [1, 64]]), LR,
           [(i * 64, i * 128) for i in range(16)], wvuTh, wvuTl,
           load_cols=1024, split=False)


    out_total = [0]
    acc1_tok = ("d0", 0)
    pending_out = [None]
    acc1_box = [("d0", 0)]

    def emit_out(s_, half_):
        row0 = s_ * T + half_ * 1024
        out_total[0] = b.q("sync", "dma_start", inc=("so", 16),
                           wait=[("so", out_total[0])],
                           out=bass.AP(yd, row0 * D, [[1024, 128], [128 * 1024, 8], [1, 1024]]),
                           in_=acc[:, :])
        if half_ == 0:
            row1 = s_ * T + 1024
            acc1_box[0] = b.dma("sync", out=acc[:, :],
                                in_=bass.AP(xd, row1 * D, [[1024, 128], [128 * 1024, 8], [1, 1024]]),
                                wait=[("so", out_total[0])])
        else:
            acc1_box[0] = ("d0", 0)
        pending_out[0] = None

    for s in range(NSEQ):
        lnsc = contextlib.ExitStack()
        ps_ln = b.psum(f"ps_ln{s}", [128, 1024], scope=lnsc)
        ps_kv = b.psum(f"ps_kv{s}", [LR, 128], scope=lnsc)
        if s > 0:
            b.new_phase()
        if pending_out[0] is not None:
            emit_out(*pending_out[0])
        v_evac_prev = 0
        lnx_free = 0
        st_b_prev = 0
        v_kv_prev = 0
        for t in range(16):
            C = lncols
            tok = t * 128
            t_x = b.dma("sync", out=lnx[:, :], in_=x[s * T + tok: s * T + tok + 128, :],
                        wait=[("sv", lnx_free)])
            w1 = [t_x]
            if st_b_prev:
                w1.append(("st", st_b_prev))
            v1 = b.q("vector", "tensor_scalar", xntmp[:, :], lnx[:, :], 1.0, 0.0, OP.mult, OP.add,
                     accum_out=C[:, 0:1], wait=w1, inc=("sv", 1))
            v2 = b.q("vector", "tensor_tensor", xntmp[:, :], lnx[:, :], lnx[:, :], OP.mult,
                     wait=[("sv", v1)], inc=("sv", 1))
            v3 = b.q("vector", "tensor_scalar", xntmp[:, :], xntmp[:, :], 1.0, 0.0, OP.mult, OP.add,
                     accum_out=C[:, 1:2], wait=[("sv", v2)], inc=("sv", 1))
            v4 = b.q("vector", "tensor_scalar", C[:, 2:3], C[:, 0:1], -1.0 / D, None, OP.mult,
                     wait=[("sv", v3)], inc=("sv", 1))
            v5 = b.q("vector", "tensor_scalar", C[:, 3:4], C[:, 1:2], 1.0 / D, None, OP.mult,
                     wait=[("sv", v3)], inc=("sv", 1))
            b.q("vector", "tensor_tensor", C[:, 4:5], C[:, 2:3], C[:, 2:3], OP.mult,
                wait=[("sv", v4)])
            v_var = b.q("vector", "tensor_tensor", C[:, 3:4], C[:, 3:4], C[:, 4:5],
                        OP.subtract, wait=[("sv", v5)], inc=("sv", 1))
            s_sqrt = b.q("scalar", "activation", C[:, 5:6], C[:, 3:4], AF.Sqrt,
                         bias=cols[:, 6:7], wait=[("sv", v_var)], inc=("ss", 1))
            vr = b.q("vector", "reciprocal", C[:, 5:6], C[:, 5:6], wait=[("ss", s_sqrt)],
                     inc=("sv", 1))
            v_xn = b.q("vector", "tensor_scalar", xntmp[:, :], lnx[:, :], C[:, 2:3],
                       C[:, 5:6], OP.add, OP.mult, wait=[("sv", vr)], inc=("sv", 1))
            if t < 8:
                lnx_free = b.q("vector", "tensor_copy", acc[:, t * D:(t + 1) * D], lnx[:, :],
                               wait=[("so", out_total[0]), ("sv", v_xn)], inc=("sv", 1))
            else:
                lnx_free = v_xn
            st_b = []
            for c in range(8):
                w8 = [("sv", v_xn)]
                if c == 0:
                    w8 += [("sv", v_evac_prev)]
                tgt = b.q("tensor", "transpose", ps_ln[:, c * 128:(c + 1) * 128],
                          xntmp[:, c * 128:(c + 1) * 128], idn[:, :], wait=w8,
                          inc=("st", 1) if c in (3, 7) else None)
                if tgt is not None:
                    st_b.append(tgt)
            st_b_prev = st_b[1]
            v_evac = 0
            xrow = 8 * T
            for bk in range(2):
                w8 = [("st", st_b[bk])]
                c0 = bk * 4
                _, v_evac = b.esplit(
                    ps_ln[:, c0 * 128:(c0 + 4) * 128],
                    bass.AP(xnh, c0 * T + tok, [[xrow, 128], [T, 4], [1, 128]]),
                    bass.AP(xnl, c0 * T + tok, [[xrow, 128], [T, 4], [1, 128]]),
                    wait=w8)
            v_evac_prev = v_evac
            st_kv = None
            nmm = 0
            for c in range(8):
                for (wt, xt_) in ((wkvTh, xnh), (wkvTh, xnl), (wkvTl, xnh)):
                    w8 = []
                    if nmm == 0:
                        w8 = [("sv", max(v_evac, v_kv_prev, v_wkv))]
                    st_kv = b.q("tensor", "matmul", ps_kv[:, :], wt[:, c * LR:(c + 1) * LR],
                                xt_[:, c * T + tok: c * T + tok + 128], wait=w8,
                                start=(nmm == 0), stop=(nmm == 23),
                                inc=("st", 1) if nmm == 23 else None)
                    nmm += 1
            _, v_kv_prev = b.esplit(ps_kv[:, :], kvh[0:LR, tok:tok + 128],
                                    kvl[0:LR, tok:tok + 128], wait=[("st", st_kv)])
            if s == 0:
                wf_emit(12)
        if s == 0:
            wf_emit(len(wf_chunks))
            b.qwait("sync", [(f"d{i}", b.ct[f"d{i}"]) for i in range(NDMA_SEM)])
            b.qwait("sync", [(f"g{i}", b.ct[f"g{i}"]) for i in range(8)])
        b.flush(f"LN{s}")
        lnsc.close()
        if s == 0:
            wsc.close()
            wfsc.close()

        for half in range(2):
            hbase = half * 1024
            acc1_tok = acc1_box[0]
            asc = contextlib.ExitStack()
            Kfh = b.sbuf(f"Kfh{s}{half}", [128, 2 * T], scope=asc, dtype=F16)
            Kfl = b.sbuf(f"Kfl{s}{half}", [128, 2 * T], scope=asc, dtype=F16)
            Vth = b.sbuf(f"Vth{s}{half}", [128, 16 * DH], scope=asc, dtype=F16)
            Vtl = b.sbuf(f"Vtl{s}{half}", [128, 16 * DH], scope=asc, dtype=F16)
            Pb = b.sbuf(f"Pb{s}{half}", [128, T], scope=asc)
            wqTh = b.sbuf(f"wqTh{s}{half}", [128, 8 * DH], scope=asc, dtype=F16)
            wqTl = b.sbuf(f"wqTl{s}{half}", [128, 8 * DH], scope=asc, dtype=F16)
            woTh = b.sbuf(f"woTh{s}{half}", [128, 2 * D], scope=asc, dtype=F16)
            woTl = b.sbuf(f"woTl{s}{half}", [128, 2 * D], scope=asc, dtype=F16)
            Qfh = b.sbuf(f"Qfh{s}{half}", [128, 2 * 1024], scope=asc, dtype=F16)
            Qfl = b.sbuf(f"Qfl{s}{half}", [128, 2 * 1024], scope=asc, dtype=F16)
            PTh = b.sbuf(f"PTh{s}{half}", [128, T], scope=asc, dtype=F16)
            PTl = b.sbuf(f"PTl{s}{half}", [128, T], scope=asc, dtype=F16)
            osh = b.sbuf(f"osh{s}{half}", [128, DH], scope=asc, dtype=F16)
            osl = b.sbuf(f"osl{s}{half}", [128, DH], scope=asc, dtype=F16)
            ps_S = b.psum(f"psS{s}{half}", [128, 2048], scope=asc)
            ps_T = b.psum(f"psT{s}{half}", [128, 2048], scope=asc)

            b.off = "att" not in stages
            b.new_phase()
            if pending_out[0] is not None and not b.off:
                emit_out(*pending_out[0])
                acc1_tok = acc1_box[0]
            b.flush(f"ATP{s}{half}")

            for h in (range(H) if "att" in stages else ()):
                b.new_phase()
                t_wqh = b.dma("sync", out=wqTh[:, :], in_=wqTh_d.ap()[h * 128:(h + 1) * 128, :])
                t_wql = b.dma("sync", out=wqTl[:, :], in_=wqTl_d.ap()[h * 128:(h + 1) * 128, :])
                t_woh = b.dma("sync", out=woTh[:, :], in_=woTh_d.ap()[h * 128:(h + 1) * 128, :])
                t_wol = b.dma("sync", out=woTl[:, :], in_=woTl_d.ap()[h * 128:(h + 1) * 128, :])
                vk_tok = [0, 0]
                vv_tok = [0, 0]
                qf_tok = [0] * 4

                def kfm_wave(dk):
                    w8 = [("sv", vk_tok[0])] if dk == 1 else []
                    wcol = (2 * h + dk) * 128
                    last = 0
                    for kt in range(4):
                        st_k = None
                        nmm = 0
                        for (wt, kt_) in ((wkuTh, kvh), (wkuTh, kvl), (wkuTl, kvh)):
                            st_k = b.q("tensor", "matmul", ps_S[:, kt * 512:(kt + 1) * 512],
                                       wt[:, wcol:wcol + 128],
                                       kt_[:, kt * 512:(kt + 1) * 512],
                                       wait=w8 if (kt, nmm) == (0, 0) else (),
                                       start=(nmm == 0), stop=(nmm == 2),
                                       inc=("st", 1) if nmm == 2 else None)
                            nmm += 1
                        col = dk * T + kt * 512
                        _, last = b.esplit(ps_S[:, kt * 512:(kt + 1) * 512],
                                           Kfh[:, col:col + 512], Kfl[:, col:col + 512],
                                           wait=[("st", st_k)])
                    vk_tok[dk] = last

                def vtm_wave(wave):
                    w8 = [("sv", vv_tok[0])] if wave == 1 else []
                    st_v = {}
                    for i in range(8):
                        kt16 = wave * 8 + i
                        kcol = kt16 * 128
                        nmm = 0
                        for (kt_, wt) in ((kvh, wvuTh), (kvh, wvuTl), (kvl, wvuTh)):
                            st_v[i] = b.q("tensor", "matmul", ps_T[:, i * DH:(i + 1) * DH],
                                          kt_[:, kcol:kcol + 128],
                                          wt[:, h * DH:(h + 1) * DH],
                                          wait=w8 if (i, nmm) == (0, 0) else (),
                                          start=(nmm == 0), stop=(nmm == 2),
                                          inc=("st", 1) if (i % 2 == 1 and nmm == 2) else None)
                            nmm += 1
                    last = 0
                    for i in range(4):
                        col = (wave * 8 + i * 2) * DH
                        _, last = b.esplit(ps_T[:, i * 2 * DH:(i * 2 + 2) * DH],
                                           Vth[:, col:col + 512], Vtl[:, col:col + 512],
                                           wait=[("st", st_v[i * 2 + 1])])
                    vv_tok[wave] = last

                def qfm_group(g):
                    dqc, tt = g // 2, g % 2
                    st_qf = None
                    nmm = 0
                    for c in range(8):
                        for (wt, xt_) in ((wqTh, xnh), (wqTh, xnl), (wqTl, xnh)):
                            w8 = []
                            if g == 0 and nmm == 0:
                                w8 = [("sv", vk_tok[1]), t_wqh, t_wql]
                            st_qf = b.q("tensor", "matmul",
                                        ps_S[:, g * 512:(g + 1) * 512],
                                        wt[:, c * DH + dqc * 128: c * DH + dqc * 128 + 128],
                                        xt_[:, c * T + hbase + tt * 512: c * T + hbase + tt * 512 + 512],
                                        wait=w8, start=(nmm == 0), stop=(nmm == 23),
                                        inc=("st", 1) if nmm == 23 else None)
                            nmm += 1
                    col = dqc * 1024 + tt * 512
                    _, qf_tok[g] = b.esplit(ps_S[:, g * 512:(g + 1) * 512],
                                            Qfh[:, col:col + 512], Qfl[:, col:col + 512],
                                            wait=[("st", st_qf)])

                kfm_wave(0)
                vtm_wave(0)
                kfm_wave(1)
                vtm_wave(1)
                for g in range(4):
                    qfm_group(g)
                v_qf = qf_tok[3]
                v_v = vv_tok[1]

                NQ = 8
                psets = (ps_S, ps_T)
                I = [dict() for _ in range(NQ)]

                def emit_S_kt(k, kt, last):
                    d = I[k]
                    pset = psets[k % 2]
                    qdat = max(qf_tok[k // 4], qf_tok[2 + k // 4])
                    if k == 0:
                        w8 = [("sv", max(qf_tok[kt], qdat if kt == 0 else 0))]
                    elif k == 1:
                        w8 = [("sv", max(vv_tok[1], qdat))] if kt == 0 else ()
                    else:
                        key = {3: "pt2", 0: "v_o", 1: "stt", 2: None}[kt]
                        w8 = [("sv", I[k - 2][key])] if key else []
                    nmm = 0
                    st_sc = None
                    for dqc in range(2):
                        qcol = dqc * 1024 + k * 128
                        kcol = dqc * T + kt * 512
                        for (qt, kt_t) in ((Qfh, Kfh), (Qfh, Kfl), (Qfl, Kfh)):
                            st_sc = b.q("tensor", "matmul",
                                        pset[:, kt * 512:(kt + 1) * 512],
                                        qt[:, qcol:qcol + 128],
                                        kt_t[:, kcol:kcol + 512],
                                        wait=w8 if nmm == 0 else (),
                                        start=(nmm == 0), stop=(nmm == 5),
                                        inc=("st", 1) if (last and nmm == 5) else None)
                            nmm += 1
                    if last:
                        d["st_sc"] = st_sc

                def emit_smax(k):
                    d = I[k]
                    pset = psets[k % 2]
                    par = (k % 2) * 2
                    w8 = [("st", d["st_sc"])]
                    if k >= 1:
                        w8 += [("st", I[k - 1]["st_tpB"]), ("sv", I[k - 1]["v_sum"])]
                    ss_exp = b.q("scalar", "activation", Pb[:, :], pset[:, :], AF.Exp,
                                 bias=cols[:, 7:8], scale=SCALE, wait=w8)
                    d["ss_exp"] = ss_exp
                    d["v_sum"] = b.q("vector", "reduce_sum", sums[:, par:par + 1], Pb[:, :],
                                     axis=AX.X, wait=[("ss", ss_exp)])
                    d["v_rq"] = b.q("vector", "reciprocal", sums[:, par + 1:par + 2],
                                    sums[:, par:par + 1])

                def emit_B_pre(k):
                    d = I[k]
                    pset = psets[k % 2]
                    st_tp = {}
                    for kc in range(16):
                        w8 = [("ss", d["ss_exp"])] if kc == 0 else ()
                        tgt = b.q("tensor", "transpose", pset[:, kc * 128:(kc + 1) * 128],
                                  Pb[:, kc * 128:(kc + 1) * 128], idn[:, :], wait=w8,
                                  inc=("st", 1) if kc % 4 == 3 else None)
                        if kc % 4 == 3:
                            st_tp[kc // 4] = tgt
                    d["st_tpB"] = st_tp[3]
                    pt_sv = {}
                    for bk in range(4):
                        _, pt_sv[bk] = b.esplit(pset[:, bk * 512:(bk + 1) * 512],
                                                PTh[:, bk * 512:(bk + 1) * 512],
                                                PTl[:, bk * 512:(bk + 1) * 512],
                                                wait=[("st", st_tp[bk])])
                    d["pt2"] = pt_sv[3]
                    st_av = None
                    nmm = 0
                    for dc in range(2):
                        for kc in range(16):
                            for (vt_, pt_) in ((Vth, PTh), (Vth, PTl), (Vtl, PTh)):
                                w8 = ()
                                if dc == 0 and kc % 4 == 0 and nmm % 3 == 0:
                                    w8 = [("sv", pt_sv[kc // 4])]
                                st_av = b.q("tensor", "matmul",
                                            pset[:, dc * 128:(dc + 1) * 128],
                                            vt_[:, kc * DH + dc * 128: kc * DH + dc * 128 + 128],
                                            pt_[:, kc * 128:(kc + 1) * 128], wait=w8,
                                            start=(kc == 0 and nmm % 3 == 0),
                                            stop=(kc == 15 and nmm % 3 == 2),
                                            inc=("st", 1) if (dc, kc, nmm % 3) == (1, 15, 2) else None)
                                nmm += 1
                    _, v_o = b.esplit(pset[:, 0:DH], osh[:, 0:DH], osl[:, 0:DH],
                                      wait=[("st", st_av)])
                    d["v_o"] = v_o

                def emit_B_post(k):
                    d = I[k]
                    pset = psets[k % 2]
                    til = k
                    st_wo = None
                    for dmh in range(2):
                        nmm = 0
                        for dc in range(2):
                            for (ot, wt) in ((osh, woTh), (osh, woTl), (osl, woTh)):
                                w8 = [("sv", d["v_o"])] if (dmh, nmm) == (0, 0) else ()
                                if (k, dmh, nmm) == (0, 0, 0):
                                    w8 = list(w8) + [t_woh, t_wol]
                                st_wo = b.q("tensor", "matmul",
                                            pset[:, 512 + dmh * 512: 512 + (dmh + 1) * 512],
                                            ot[:, dc * 128:(dc + 1) * 128],
                                            wt[:, dc * D + dmh * 512: dc * D + (dmh + 1) * 512],
                                            wait=w8, start=(nmm == 0), stop=(nmm == 5),
                                            inc=("st", 1) if (dmh, nmm) == (1, 5) else None)
                                nmm += 1
                    par = (k % 2) * 2
                    v_stt = None
                    for dmh in range(2):
                        w8 = []
                        if dmh == 0:
                            w8 = [("st", st_wo), ("sv", d["v_rq"]), acc1_tok]
                        v_stt = b.q("vector", "scalar_tensor_tensor",
                                    acc[:, til * D + dmh * 512: til * D + (dmh + 1) * 512],
                                    pset[:, 512 + dmh * 512: 512 + (dmh + 1) * 512],
                                    sums[:, par + 1:par + 2],
                                    acc[:, til * D + dmh * 512: til * D + (dmh + 1) * 512],
                                    OP.mult, OP.add, wait=w8, inc=("sv", 1))
                    d["stt"] = v_stt

                for slot in range(NQ + 2):
                    k, kb = slot, slot - 2
                    if kb >= 0:
                        emit_B_pre(kb)
                    if k < NQ and k >= 2:
                        emit_S_kt(k, 3, last=False)
                    if kb >= 0:
                        emit_B_post(kb)
                    if 1 <= slot <= NQ:
                        emit_smax(slot - 1)
                    if k < NQ:
                        if k >= 2:
                            for kt in (0, 1, 2):
                                emit_S_kt(k, kt, last=(kt == 2))
                        else:
                            for kt in (0, 1, 2, 3):
                                emit_S_kt(k, kt, last=(kt == 3))
                b.flush(f"AT{s}{half}h{h}")
            b.off = False
            asc.close()

            fsc = contextlib.ExitStack()
            ffnwh = b.sbuf(f"ffnwh{s}{half}", [128, 8192], scope=fsc, dtype=F16)
            ffnwl = b.sbuf(f"ffnwl{s}{half}", [128, 8192], scope=fsc, dtype=F16)
            stg32 = b.sbuf(f"stg32{s}{half}", [128, 4 * 1024], scope=fsc)
            hqbh = b.sbuf(f"hqbh{s}{half}", [128, 4 * 1024], scope=fsc, dtype=F16)
            hqbl = b.sbuf(f"hqbl{s}{half}", [128, 4 * 1024], scope=fsc, dtype=F16)
            tmp5 = b.sbuf(f"tmp5{s}{half}", [128, 512], scope=fsc)
            ones_row = b.sbuf(f"ones{s}{half}", [1, 128], scope=fsc, dtype=F16)
            b2rh = b.sbuf(f"b2rh{s}{half}", [1, D], scope=fsc, dtype=F16)
            b2rl = b.sbuf(f"b2rl{s}{half}", [1, D], scope=fsc, dtype=F16)
            v_ones = [0]
            tb2 = [("d0", 0), ("d0", 0)]
            ps_f1 = b.psum(f"psf1{s}{half}", [128, 2048], scope=fsc)
            ps_f2 = b.psum(f"psf2{s}{half}", [128, 2048], scope=fsc)
            ps_of = (ps_f1, ps_f2)

            pf = {}
            mjobs = moe_jobs(half)
            if "ffnsh" in stages:
              for hq in range(4):
                b.new_phase()
                if pending_out[0] is not None:
                    emit_out(*pending_out[0])
                    acc1_tok = acc1_box[0]
                if hq == 0:
                    tb2[0] = b.dma("sync", out=b2rh[0:1, :],
                                   in_=bass.AP(sb2h_d, 0, [[1, 1], [1, D]]))
                    tb2[1] = b.dma("sync", out=b2rl[0:1, :],
                                   in_=bass.AP(sb2l_d, 0, [[1, 1], [1, D]]))
                    v_ones[0] = b.q("vector", "memset", ones_row[:, :], 1.0)

                def zmm(tw, vprev):
                    close = {}
                    chunk_close = [0] * 8
                    for c in range(8):
                        for tt in range(2):
                            psf = ps_of[tt]
                            for j in range(4):
                                lhs_col = c * 512 + j * 128
                                rhs_col = c * T + hbase + tt * 512
                                nmm = 0
                                for (wt, xt_) in ((ffnwh, xnh), (ffnwh, xnl), (ffnwl, xnh)):
                                    w8 = []
                                    if (tt, j, nmm) == (0, 0, 0):
                                        w8.extend(tw[c])
                                    if c == 0 and nmm == 0 and vprev is not None:
                                        w8.append(("ss", vprev[(tt, j)]))
                                    inc = None
                                    if (c == 7 or (tt, j) == (1, 3)) and nmm == 2:
                                        inc = ("st", 1)
                                    tgt = b.q("tensor", "matmul", psf[:, j * 512:(j + 1) * 512],
                                              wt[:, lhs_col:lhs_col + 128],
                                              xt_[:, rhs_col:rhs_col + 512],
                                              wait=w8, start=(c == 0 and nmm == 0),
                                              stop=(c == 7 and nmm == 2), inc=inc)
                                    nmm += 1
                                if c == 7:
                                    close[(tt, j)] = tgt
                                if (tt, j) == (1, 3):
                                    chunk_close[c] = tgt
                    return close, chunk_close

                def wload(cols_out, src_h, src_l, wait=()):
                    th = b.dma("sync", out=ffnwh[:, cols_out], in_=src_h, wait=wait)
                    tl = b.dma("sync", out=ffnwl[:, cols_out], in_=src_l, wait=wait)
                    return [th, tl]

                tw1 = [pf.pop(c, None) or
                       wload(slice(c * 512, (c + 1) * 512),
                             sw1h_d.ap()[c * 128:(c + 1) * 128, hq * 512:(hq + 1) * 512],
                             sw1l_d.ap()[c * 128:(c + 1) * 128, hq * 512:(hq + 1) * 512])
                       for c in range(8)]
                close1, cc1 = zmm(tw1, None)
                v_z1e = {}
                for tt in range(2):
                    psf = ps_of[tt]
                    for j in range(4):
                        bcol = sb1c[:, hq * 4 + j: hq * 4 + j + 1]
                        v_z1e[(tt, j)] = b.q(
                            "scalar", "activation",
                            stg32[:, j * 1024 + tt * 512: j * 1024 + (tt + 1) * 512],
                            psf[:, j * 512:(j + 1) * 512],
                            AF.Silu, bias=bcol,
                            wait=[("st", close1[(tt, j)])], inc=("ss", 1))
                tw3 = [wload(slice(c * 512, (c + 1) * 512),
                             sw3h_d.ap()[c * 128:(c + 1) * 128, hq * 512:(hq + 1) * 512],
                             sw3l_d.ap()[c * 128:(c + 1) * 128, hq * 512:(hq + 1) * 512],
                             wait=[("st", cc1[c])])
                       for c in range(8)]
                close3, cc3 = zmm(tw3, v_z1e)
                v_z3 = 0
                hq_e = {}
                for j in range(4):
                    for tt in range(2):
                        psf = ps_of[tt]
                        hcol = j * 1024 + tt * 512
                        bcol = sb3c[:, hq * 4 + j: hq * 4 + j + 1]
                        vts3 = b.q("vector", "tensor_scalar", tmp5[:, :], psf[:, j * 512:(j + 1) * 512],
                                   bcol, None, OP.add,
                                   wait=[("st", close3[(tt, j)]), ("sv", v_z3)], inc=("sv", 1))
                        v_z3 = b.q("vector", "tensor_tensor",
                                   stg32[:, hcol:hcol + 512],
                                   stg32[:, hcol:hcol + 512],
                                   tmp5[:, :], OP.mult, wait=[("sv", vts3)], inc=("sv", 1))
                        ss_hh = b.q("scalar", "copy", hqbh[:, hcol:hcol + 512],
                                    stg32[:, hcol:hcol + 512], wait=[("sv", v_z3)])
                        v_z3 = b.q("vector", "tensor_tensor", hqbl[:, hcol:hcol + 512],
                                   stg32[:, hcol:hcol + 512], hqbh[:, hcol:hcol + 512],
                                   OP.subtract, wait=[("ss", ss_hh)])
                        hq_e[(tt, j)] = v_z3
                tw2 = [wload(slice(jj * 1024, (jj + 1) * 1024),
                             sw2h_d.ap()[hq * 512 + jj * 128: hq * 512 + (jj + 1) * 128, :],
                             sw2l_d.ap()[hq * 512 + jj * 128: hq * 512 + (jj + 1) * 128, :],
                             wait=[("st", cc3[2 * jj + 1])])
                       for jj in range(4)]
                groups = [(tt16, ch) for tt16 in range(8) for ch in range(2)]
                stt_tgt = [0] * 8
                fs_jj_tok = [0] * 4
                for w in range(2):
                    wave = groups[w * 8:(w + 1) * 8]
                    close_w = [None] * 8
                    for jj in range(4):
                        for gi, (tt16, ch) in enumerate(wave):
                            psf = ps_of[gi // 4]
                            pslice = psf[:, (gi % 4) * 512:((gi % 4) + 1) * 512]
                            lhs_col = jj * 1024 + tt16 * 128
                            rhs_col = jj * 1024 + ch * 512
                            nmm = 0
                            for (ht, wt) in ((hqbh, ffnwh), (hqbh, ffnwl), (hqbl, ffnwh)):
                                w8 = []
                                if gi == 0 and nmm == 0:
                                    w8.extend(tw2[jj])
                                    if w == 0:
                                        w8.append(("sv", max(hq_e[(0, jj)], hq_e[(1, jj)])))
                                if jj == 0 and nmm == 0:
                                    if w == 0:
                                        w8.append(("sv", hq_e[(gi // 4, gi % 4)]))
                                    if w == 1:
                                        w8.append(("sv", stt_tgt[gi]))
                                last = (jj == 3 and nmm == 2 and hq != 0)
                                incw = ("st", 1) if (last or (w == 1 and gi == 7 and nmm == 2)) else None
                                close_w[gi] = b.q("tensor", "matmul", pslice,
                                                  ht[:, lhs_col:lhs_col + 128],
                                                  wt[:, rhs_col:rhs_col + 512],
                                                  wait=w8, start=(jj == 0 and nmm == 0),
                                                  stop=last, inc=incw)
                                nmm += 1
                            if w == 1 and gi == 7:
                                fs_jj_tok[jj] = b.ct["st"]
                    if hq == 0:
                        for gi, (tt16, ch) in enumerate(wave):
                            psf = ps_of[gi // 4]
                            pslice = psf[:, (gi % 4) * 512:((gi % 4) + 1) * 512]
                            b.q("tensor", "matmul", pslice, ones_row[0:1, :],
                                b2rh[0:1, ch * 512:(ch + 1) * 512],
                                wait=[("sv", v_ones[0]), tb2[0], tb2[1]] if (w, gi) == (0, 0) else (),
                                start=False, stop=False)
                            close_w[gi] = b.q("tensor", "matmul", pslice, ones_row[0:1, :],
                                              b2rl[0:1, ch * 512:(ch + 1) * 512],
                                              start=False, stop=True, inc=("st", 1))
                    for gi, (tt16, ch) in enumerate(wave):
                        psf = ps_of[gi // 4]
                        pslice = psf[:, (gi % 4) * 512:((gi % 4) + 1) * 512]
                        stt_tgt[gi] = b.q("vector", "scalar_tensor_tensor",
                                          acc[:, tt16 * D + ch * 512: tt16 * D + (ch + 1) * 512],
                                          pslice, 0.5,
                                          acc[:, tt16 * D + ch * 512: tt16 * D + (ch + 1) * 512],
                                          OP.mult, OP.add,
                                          wait=[("st", close_w[gi]), acc1_tok], inc=("sv", 1))
                if hq < 3:
                    for c in range(8):
                        pf[c] = wload(slice(c * 512, (c + 1) * 512),
                                      sw1h_d.ap()[c * 128:(c + 1) * 128, (hq + 1) * 512:(hq + 2) * 512],
                                      sw1l_d.ap()[c * 128:(c + 1) * 128, (hq + 1) * 512:(hq + 2) * 512],
                                      wait=[("st", fs_jj_tok[c // 2])])
                elif "moe" in stages and mjobs:
                    e0 = mjobs[0][0]
                    for c in range(8):
                        pf[c] = wload(slice(c * 1024, (c + 1) * 1024),
                                      w1h_d.ap()[e0, c * 128:(c + 1) * 128, :],
                                      w1l_d.ap()[e0, c * 128:(c + 1) * 128, :],
                                      wait=[("st", fs_jj_tok[c])] if c < 4 else ())
                b.flush(f"FS{s}{half}q{hq}")

            for ji, (e, rs) in enumerate(mjobs if "moe" in stages else ()):
                ngrp = len(rs)
                b.new_phase()
                if pending_out[0] is not None:
                    emit_out(*pending_out[0])
                    acc1_tok = acc1_box[0]
                if not v_ones[0]:
                    v_ones[0] = b.q("vector", "memset", ones_row[:, :], 1.0)
                tw1m = [pf.pop(c, None) or
                        wload(slice(c * 1024, (c + 1) * 1024),
                              w1h_d.ap()[e, c * 128:(c + 1) * 128, :],
                              w1l_d.ap()[e, c * 128:(c + 1) * 128, :])
                        for c in range(8)]
                close_j = [None] * 8
                chunk_close = [0] * 8
                for c in range(8):
                    for j in range(8):
                        psf = ps_of[j // 4]
                        for g, r in enumerate(rs):
                            rcol = (r - half * 4) * 256
                            nmm = 0
                            for (wt, xt_) in ((ffnwh, xnh), (ffnwh, xnl), (ffnwl, xnh)):
                                w8 = []
                                if (j, g, nmm) == (0, 0, 0):
                                    w8.extend(tw1m[c])
                                inc = None
                                if c == 7 and g == ngrp - 1 and nmm == 2:
                                    inc = ("st", 1)
                                if (j, g, nmm) == (7, ngrp - 1, 2):
                                    inc = ("st", 1)
                                tgt = b.q("tensor", "matmul",
                                          psf[:, (j % 4) * 512 + g * 256: (j % 4) * 512 + (g + 1) * 256],
                                          wt[:, c * 1024 + j * 128: c * 1024 + j * 128 + 128],
                                          xt_[:, c * T + hbase + rcol: c * T + hbase + rcol + 256],
                                          wait=w8, start=(c == 0 and g == 0 and nmm == 0),
                                          stop=(c == 7 and g == ngrp - 1 and nmm == 2), inc=inc)
                                nmm += 1
                            if c == 7 and g == ngrp - 1:
                                close_j[j] = tgt
                            if (j, g) == (7, ngrp - 1):
                                chunk_close[c] = tgt
                hm_e = [0] * 8
                ss_gj = [0] * 8
                for j in range(8):
                    psf = ps_of[j // 4]
                    w_j = j * 512
                    n_j = ngrp * 256
                    ss_gj[j] = b.q("scalar", "activation",
                                   stg32[:, w_j: w_j + n_j],
                                   psf[:, (j % 4) * 512: (j % 4) * 512 + n_j],
                                   AF.Gelu, bias=b1c[:, e * 8 + j: e * 8 + j + 1],
                                   wait=[("st", close_j[j])], inc=("ss", 1))
                    v_hh = b.q("vector", "tensor_copy", hqbh[:, w_j: w_j + n_j],
                               stg32[:, w_j: w_j + n_j], wait=[("ss", ss_gj[j])], inc=("sv", 1))
                    hm_e[j] = b.q("vector", "tensor_tensor", hqbl[:, w_j: w_j + n_j],
                                  stg32[:, w_j: w_j + n_j], hqbh[:, w_j: w_j + n_j],
                                  OP.subtract, wait=[("sv", v_hh)], inc=("sv", 1))
                tw2m = [wload(slice(c * 1024, (c + 1) * 1024),
                              w2h_d.ap()[e, c * 128:(c + 1) * 128, :],
                              w2l_d.ap()[e, c * 128:(c + 1) * 128, :],
                              wait=[("st", chunk_close[c])])
                        for c in range(8)]
                t_b2h = b.dma("sync", out=b2rh[0:1, :],
                              in_=bass.AP(b2h_d, e * D, [[1, 1], [1, D]]))
                t_b2l = b.dma("sync", out=b2rl[0:1, :],
                              in_=bass.AP(b2l_d, e * D, [[1, 1], [1, D]]))
                wgroups = [(g, sub, ch) for g in range(ngrp) for sub in range(2) for ch in range(2)]
                close_w = [None] * len(wgroups)
                mo_j_tok = [0] * 8
                for j in range(8):
                    for gi, (g, sub, ch) in enumerate(wgroups):
                        psf = ps_of[gi // 4]
                        pslice = psf[:, (gi % 4) * 512:((gi % 4) + 1) * 512]
                        lhs_col = j * 512 + g * 256 + sub * 128
                        rhs_col = j * 1024 + ch * 512
                        nmm = 0
                        for (ht, wt) in ((hqbh, ffnwh), (hqbh, ffnwl), (hqbl, ffnwh)):
                            w8 = []
                            if gi == 0 and nmm == 0:
                                w8.extend(tw2m[j])
                                w8.append(("sv", hm_e[j]))
                            if j == 0 and nmm == 0:
                                w8.append(("ss", ss_gj[gi]))
                            lastg = (gi == len(wgroups) - 1 and nmm == 2)
                            close_w[gi] = b.q("tensor", "matmul", pslice,
                                              ht[:, lhs_col:lhs_col + 128],
                                              wt[:, rhs_col:rhs_col + 512],
                                              wait=w8, start=(j == 0 and nmm == 0), stop=False,
                                              inc=("st", 1) if lastg else None)
                            nmm += 1
                        if gi == len(wgroups) - 1:
                            mo_j_tok[j] = b.ct["st"]
                for gi, (g, sub, ch) in enumerate(wgroups):
                    psf = ps_of[gi // 4]
                    pslice = psf[:, (gi % 4) * 512:((gi % 4) + 1) * 512]
                    b.q("tensor", "matmul", pslice, ones_row[0:1, :],
                        b2rh[0:1, ch * 512:(ch + 1) * 512],
                        wait=([t_b2h, t_b2l, ("sv", v_ones[0])] if gi == 0 else ()),
                        start=False, stop=False)
                    close_w[gi] = b.q("tensor", "matmul", pslice, ones_row[0:1, :],
                                      b2rl[0:1, ch * 512:(ch + 1) * 512],
                                      start=False, stop=True, inc=("st", 1))
                for gi, (g, sub, ch) in enumerate(wgroups):
                    psf = ps_of[gi // 4]
                    pslice = psf[:, (gi % 4) * 512:((gi % 4) + 1) * 512]
                    til = (rs[g] - half * 4) * 2 + sub
                    b.q("vector", "scalar_tensor_tensor",
                        acc[:, til * D + ch * 512: til * D + (ch + 1) * 512],
                        pslice, 0.25,
                        acc[:, til * D + ch * 512: til * D + (ch + 1) * 512],
                        OP.mult, OP.add, wait=[("st", close_w[gi]), acc1_tok], inc=("sv", 1))
                if ji + 1 < len(mjobs):
                    e_nx = mjobs[ji + 1][0]
                    for c in range(8):
                        pf[c] = wload(slice(c * 1024, (c + 1) * 1024),
                                      w1h_d.ap()[e_nx, c * 128:(c + 1) * 128, :],
                                      w1l_d.ap()[e_nx, c * 128:(c + 1) * 128, :],
                                      wait=[("st", mo_j_tok[c])])
                b.flush(f"MO{s}{half}e{e}")
            fsc.close()

            pending_out[0] = (s, half)
            if (s, half) == (NSEQ - 1, 1):
                b.new_phase()
                emit_out(s, half)
                b.qwait("sync", [("so", out_total[0])])
                b.flush(f"OUT{s}{half}")
    return b


_nc_cache = None


def _get_nc():
    global _nc_cache
    if _nc_cache is None:
        _nc_cache = build_nc().nc
    return _nc_cache


def _perm():
    j = np.arange(T)
    t_of_j = (j % 256) * 8 + (j // 256)
    inv = np.empty(T, np.int64)
    inv[t_of_j] = j
    return t_of_j, inv


def kernel(**inputs):
    x = np.asarray(inputs["L0_attn_x"], dtype=np.float32).reshape(B, T, D)
    t_of_j, inv = _perm()
    weights = {
        "Wq": inputs["L0_attn_Wq"], "Wkv": inputs["L0_attn_Wkv_down"],
        "Wku": inputs["L0_attn_Wk_up"], "Wvu": inputs["L0_attn_Wv_up"],
        "Wo": inputs["L0_attn_Wo"],
        "sw1": inputs["L0_ffn_sw1"], "sb1": inputs["L0_ffn_sb1"],
        "sw3": inputs["L0_ffn_sw3"], "sb3": inputs["L0_ffn_sb3"],
        "sw2": inputs["L0_ffn_sw2"], "sb2": inputs["L0_ffn_sb2"],
        "w1": inputs["L0_ffn_w1"], "b1": inputs["L0_ffn_b1"],
        "w2": inputs["L0_ffn_w2"], "b2": inputs["L0_ffn_b2"],
    }
    weights = {k: np.ascontiguousarray(np.asarray(v, dtype=np.float32)) for k, v in weights.items()}
    ident = np.eye(128, dtype=np.float32)
    nc = _get_nc()
    in_maps = []
    for c in range(8):
        xc = np.ascontiguousarray(x[2 * c:2 * c + 2][:, t_of_j, :].reshape(NTOK, D))
        im = {"x": xc, "ident": ident}
        im.update(weights)
        in_maps.append(im)
    res = run_bass_kernel_spmd(nc, in_maps, core_ids=list(range(8)))
    out = np.empty((B, T, D), dtype=np.float32)
    for c in range(8):
        yc = np.asarray(res.results[c]["y"]).reshape(NSEQ, T, D)
        out[2 * c:2 * c + 2] = yc[:, inv, :]
    return out
